# version 04
# baseline (speedup 1.0000x reference)
import math
import contextlib
import numpy as np
import ml_dtypes
import concourse.bass as bass
import concourse.mybir as mybir
from concourse.bass_utils import run_bass_kernel_spmd

F32 = mybir.dt.float32
BF16 = mybir.dt.bfloat16
I8 = mybir.dt.int8
AF = mybir.ActivationFunctionType
ALU = mybir.AluOpType
AX = mybir.AxisListType.X

B, S, D = 2, 8192, 1024
NCORE = 8
T = 2048
DPH, ROTD = 64, 32
DI, DF = 4096, 2816
QB = 512
NQB = T // QB
NKC = S // 128
NIC = DF // 128
NI1 = DI // 128
AGR = 2 * D + 1

STAGE = 9
DBG = ()


def host_rot_table(t0):
    dim_exp = ROTD // 2
    freqs = 2.0 ** np.linspace(0.0, math.log2(10.0 / 2.0), dim_exp)
    pos = (1.0 + np.arange(t0, t0 + T, dtype=np.float64))[:, None]
    rad = pos * freqs * math.pi
    sn, cs = np.sin(rad), np.cos(rad)
    snt, cst = np.tile(sn, (1, 2)), np.tile(cs, (1, 2))
    Ah = np.ones((T, DPH))
    for j in range(ROTD):
        Ah[:, j] = snt[:, j] - cst[:, j] if j % 2 == 0 else snt[:, j] + cst[:, j]
    tab = np.ones((128, T))
    tab[0:DPH, :] = Ah.T
    tab[DPH:2 * DPH, :] = Ah.T
    return np.ascontiguousarray(tab.astype(np.float32))


def bias_col_map():
    m, cc = {}, 0
    for l in range(2):
        for nm, nch in (("ln1_g", 8), ("ln1_b", 8), ("ln2_g", 8), ("ln2_b", 8),
                        ("bq", 8), ("bk", 8), ("bv", 8), ("b1", 32)):
            for c in range(nch):
                m[(l, nm, c)] = cc
                cc += 1
    return m, cc


BIAS_COLS, NBC = bias_col_map()


class P:
    ENGS = ("sync", "gpsimd", "tensor", "vector", "scalar")

    def __init__(self, nc, es):
        self.nc = nc
        self.steps = {e: [] for e in self.ENGS}
        self.cnt = {}
        self.sems = {}
        for s in ("L", "G", "C", "T", "V", "S", "F"):
            h = es.enter_context(nc.semaphore(s))
            self.sems[s] = h
            self.cnt[s] = 0
        self.nfence = 0

    def emit(self, eng, fn, waits=(), inc=None, amt=1):
        v = None
        if inc is not None:
            self.cnt[inc] += amt
            v = self.cnt[inc]
        self.steps[eng].append((fn, tuple(waits), inc, amt))
        return v

    def op(self, eng, meth, waits=(), inc="default", amt=1, sep=False, **kw):
        if inc == "default":
            inc = {"tensor": "T", "vector": "V", "scalar": "S",
                   "gpsimd": "G", "sync": "L"}[eng]
        if sep:
            self.steps[eng].append((lambda e: e.nop(), (), None, 0))

        def fn(e, meth=meth, kw=kw):
            return getattr(e, meth)(**kw)
        return self.emit(eng, fn, waits, inc, amt)

    def dma(self, eng, out, in_, waits=()):
        sem = "L" if eng == "sync" else "G"
        return self.emit(eng, lambda e, o=out, i=in_: e.dma_start(out=o, in_=i),
                         waits, sem, 16)

    def wait(self, eng, waits):
        self.steps[eng].append((None, tuple(waits), None, 0))

    def fence(self):
        self.nfence += 1
        for e in self.ENGS:
            self.cnt["F"] += 1
            self.steps[e].append(
                (lambda eng: eng.sem_inc(self.sems["F"], 1), (), None, 0))
        tgt = self.cnt["F"]
        cL, cG = self.cnt["L"], self.cnt["G"]
        for e in self.ENGS:
            self.wait(e, [("F", tgt), ("L", cL), ("G", cG)])

    def check(self):
        heads = {e: 0 for e in self.ENGS}
        counts = {s: 0 for s in self.cnt}
        progress = True
        while progress:
            progress = False
            for e in self.ENGS:
                while heads[e] < len(self.steps[e]):
                    fn, waits, inc, amt = self.steps[e][heads[e]]
                    if all(counts[s] >= v for s, v in waits):
                        if inc is not None:
                            counts[inc] += amt
                        elif fn is not None and amt == 0:
                            counts["F"] += 1
                        heads[e] += 1
                        progress = True
                    else:
                        break
        stuck = {e: heads[e] for e in self.ENGS if heads[e] < len(self.steps[e])}
        if stuck:
            det = {}
            for e in stuck:
                fn, waits, inc, amt = self.steps[e][heads[e]]
                det[e] = [(s, v, counts[s]) for s, v in waits if counts[s] < v]
            raise RuntimeError(f"schedule deadlock at {stuck}: {det}")

    def build(self, block):
        for ename in self.ENGS:
            sec = getattr(block, ename)
            steps = self.steps[ename]

            def body(eng, steps=steps):
                done = {}
                for fn, waits, inc, amt in steps:
                    ws = {}
                    for sname, v in waits:
                        if v > done.get(sname, 0):
                            ws[sname] = max(ws.get(sname, 0), v)
                    ws = sorted(ws.items())
                    while len(ws) > 2:
                        eng.wait_ge(self.sems[ws[0][0]], ws[0][1])
                        eng.wait_ge(self.sems[ws[1][0]], ws[1][1])
                        eng.nop()
                        for sname, v in ws[0:2]:
                            done[sname] = max(done.get(sname, 0), v)
                        ws = ws[2:]
                    nw = len(ws)
                    for sname, v in ws:
                        eng.wait_ge(self.sems[sname], v)
                        done[sname] = max(done.get(sname, 0), v)
                    if fn is None:
                        if nw:
                            eng.nop()
                        continue
                    if nw > 1:
                        eng.nop()
                    ins = fn(eng)
                    if inc is not None:
                        ins.then_inc(self.sems[inc], amt)
            sec(body)


def build_program(stage=STAGE, dbg=DBG):
    nc = bass.Bass("TRN2", target_bir_lowering=False, debug=False,
                   detect_race_conditions=False)

    xT = nc.dram_tensor("xT", [D, T], F32, kind="ExternalInput").ap()
    oT = nc.dram_tensor("oT", [D, T], F32, kind="ExternalOutput").ap()
    wq3 = nc.dram_tensor("wq3", [2, D, D], F32, kind="ExternalInput").ap()
    wk3 = nc.dram_tensor("wk3", [2, D, D], F32, kind="ExternalInput").ap()
    wv3 = nc.dram_tensor("wv3", [2, D, D], F32, kind="ExternalInput").ap()
    wo3 = nc.dram_tensor("wo3", [2, D, D], F32, kind="ExternalInput").ap()
    w13 = nc.dram_tensor("w13", [2, D, DI], F32, kind="ExternalInput").ap()
    w23 = nc.dram_tensor("w23", [2, DI, D], F32, kind="ExternalInput").ap()
    wg2 = nc.dram_tensor("wg2", [DF, D], F32, kind="ExternalInput").ap()
    wu2 = nc.dram_tensor("wu2", [DF, D], F32, kind="ExternalInput").ap()
    wd2 = nc.dram_tensor("wd2", [D, DF], F32, kind="ExternalInput").ap()
    biasd = nc.dram_tensor("biasd", [128, NBC], F32, kind="ExternalInput").ap()
    clipin = nc.dram_tensor("clipin", [1, 2], F32, kind="ExternalInput").ap()
    rotd = nc.dram_tensor("rotd", [128, T], F32, kind="ExternalInput").ap()
    identbd = nc.dram_tensor("identbd", [128, 128], BF16, kind="ExternalInput").ap()
    identfd = nc.dram_tensor("identfd", [128, 128], F32, kind="ExternalInput").ap()
    onesbd = nc.dram_tensor("onesbd", [128, 1], BF16, kind="ExternalInput").ap()
    onesf1d = nc.dram_tensor("onesf1d", [1, 128], F32, kind="ExternalInput").ap()

    dbg_out = {}
    def mkdbg(name, shape, dt=F32):
        if name in dbg and name not in dbg_out:
            dbg_out[name] = nc.dram_tensor("dbg_" + name, shape, dt,
                                           kind="ExternalOutput").ap()
        return dbg_out.get(name)

    ag_in = [nc.dram_tensor(f"ag_in{l}", [AGR, T], BF16).ap() for l in range(2)]
    ag_out = [nc.dram_tensor(f"ag_out{l}", [4 * AGR, T], BF16).ap() for l in range(2)]
    ar_in = [nc.dram_tensor(f"ar_in{i}", [1, 16], F32).ap() for i in range(2)]
    ar_out = [nc.dram_tensor(f"ar_out{i}", [1, 16], F32).ap() for i in range(2)]
    bar_in = [nc.dram_tensor(f"bar_in{i}", [1, 16], F32).ap() for i in range(2)]
    bar_out = [nc.dram_tensor(f"bar_out{i}", [1, 16], F32).ap() for i in range(2)]
    wgt = nc.dram_tensor("wgt", [NIC, 128, 1024], BF16).ap()
    wut = nc.dram_tensor("wut", [NIC, 128, 1024], BF16).ap()
    wdt = nc.dram_tensor("wdt", [NIC, 128, 1024], BF16).ap()
    w1s = nc.dram_tensor("w1s", [D, DI], BF16).ap()
    w2s = nc.dram_tensor("w2s", [DI, D], BF16).ap()
    h_scr = nc.dram_tensor("h_scr", [DF, T], F32).ap()
    ffn_scr = nc.dram_tensor("ffn_scr", [D, T], F32).ap()

    es = contextlib.ExitStack()
    p = P(nc, es)

    class Arena:
        def __init__(self, nc, size):
            self.nc = nc
            self.guard = es.enter_context(
                nc.sbuf_tensor("arena", [128, size], mybir.dt.uint8))
            self.base = nc.lookup_mloc(self.guard).addr
            self.addr = 0
            self.size = 0
            self.n = 0

        def sbuf_tensor(self, name, shape, dt):
            import functools
            import operator
            per = functools.reduce(operator.mul, shape[1:]) * mybir.dt.size(dt)
            a = (self.addr + 31) // 32 * 32
            self.n += 1
            h = self.nc.alloc_sbuf_tensor_at(
                f"{name}_{self.n}", shape, dt, offset=self.base + a)
            self.addr = a + per
            self.size = max(self.size, self.addr)
            return h

    ar = Arena(nc, 188 * 1024)
    def sb(name, shape, dt):
        return ar.sbuf_tensor(name, shape, dt)

    xa = sb("xa", [128, 8 * T], F32)
    qt = sb("qt", [128, 8 * T], BF16)
    da = sb("da", [128, 8 * T], BF16)
    bias_sb = sb("bias_sb", [128, NBC], F32)
    identb = sb("identb", [128, 128], BF16)
    identf = sb("identf", [128, 128], F32)
    onesb = sb("onesb", [128, 1], BF16)
    onesf1 = sb("onesf1", [1, 128], F32)
    rows = sb("rows", [1, 1536], F32)
    sc = sb("sc", [128, 16], F32)
    scr = sb("scr", [1, 64], F32)
    arst = sb("arst", [1, 16], F32)
    nrm4 = sb("nrm4", [1, 32], BF16)
    nrow = sb("nrow", [1, T], BF16)
    scratch_base = ar.addr

    def reset_scratch():
        ar.addr = scratch_base

    pes = contextlib.ExitStack()
    pA = pes.enter_context(nc.psum_tensor("pA", [128, 1024], F32)).ap()
    pB = pes.enter_context(nc.psum_tensor("pB", [128, 1024], F32)).ap()
    pC = pes.enter_context(nc.psum_tensor("pC", [128, 1024], F32)).ap()
    pD = pes.enter_context(nc.psum_tensor("pD", [128, 1024], F32)).ap()
    BANKS = [pA[:, 0:512], pA[:, 512:1024], pB[:, 0:512], pB[:, 512:1024],
             pC[:, 0:512], pC[:, 512:1024], pD[:, 0:512], pD[:, 512:1024]]

    def mm(out, lhsT, rhs, start=True, stop=True, waits=(), tp=None):
        kw = {"tile_position": tp} if tp is not None else {}
        return p.op("tensor", "matmul", waits=waits, out=out, lhsT=lhsT, rhs=rhs,
                    start=start, stop=stop, skip_group_check=True, **kw)

    def col(l, nm, c):
        i = BIAS_COLS[(l, nm, c)]
        return bias_sb[:, i:i + 1]

    p.dma("sync", bias_sb[:], biasd)
    p.dma("sync", identb[:], identbd)
    p.dma("sync", identf[:], identfd)
    p.dma("sync", onesb[:], onesbd)
    p.dma("sync", onesf1[:], onesf1d)

    def emit_ln(mode, l=None, gname=None, bname=None, dst_bf=None):
        xh = sb("ln_xh", [128, T], BF16)
        sqc = sb("ln_sq", [128, T], BF16)
        bc = sb("ln_bc", [128, 1024], F32)
        u32 = sb("ln_u", [128, QB], F32)
        last_stat = p.cnt["T"]
        for ch in range(8):
            w = (("T", last_stat),)
            vc = p.op("vector", "tensor_copy", waits=w, out=xh[:],
                      in_=xa[:, ch * T:(ch + 1) * T])
            vs = p.op("vector", "tensor_mul", out=sqc[:], in0=xh[:], in1=xh[:])
            for qb in range(NQB):
                mm(BANKS[qb * 2][0:1, :], onesb[:], xh[:, qb * QB:(qb + 1) * QB],
                   start=(ch == 0), stop=(ch == 7), waits=(("V", vc),))
                last_stat = mm(BANKS[qb * 2 + 1][0:1, :], onesb[:],
                               sqc[:, qb * QB:(qb + 1) * QB],
                               start=(ch == 0), stop=(ch == 7), waits=(("V", vs),))
        for qb in range(NQB):
            w = (("T", last_stat),)
            p.op("vector", "tensor_scalar_mul", waits=w,
                 out=rows[0:1, 0:512], in0=BANKS[qb * 2][0:1, :],
                 scalar1=1.0 / 1024.0)
            p.op("vector", "tensor_scalar_mul",
                 out=rows[0:1, 512:1024], in0=BANKS[qb * 2 + 1][0:1, :],
                 scalar1=1.0 / 1024.0)
            p.op("vector", "tensor_mul", out=bc[0:1, 0:512],
                 in0=rows[0:1, 0:512], in1=rows[0:1, 0:512])
            p.op("vector", "tensor_sub", out=rows[0:1, 512:1024],
                 in0=rows[0:1, 512:1024], in1=bc[0:1, 0:512])
            v3 = p.op("vector", "tensor_scalar_add", out=rows[0:1, 512:1024],
                      in0=rows[0:1, 512:1024], scalar1=1e-5)
            s1 = p.op("scalar", "activation", waits=(("V", v3),),
                      out=rows[0:1, 512:1024], in_=rows[0:1, 512:1024],
                      func=AF.Sqrt)
            v4 = p.op("vector", "reciprocal", waits=(("S", s1),),
                      out=rows[0:1, 1024:1536], in_=rows[0:1, 512:1024])
            t1 = mm(BANKS[qb * 2], onesf1[0:1, :], rows[0:1, 0:512],
                    waits=(("V", v4),))
            t2 = mm(BANKS[qb * 2 + 1], onesf1[0:1, :], rows[0:1, 1024:1536],
                    waits=(("V", v4),))
            p.op("vector", "tensor_copy", waits=(("T", t1),),
                 out=bc[:, 0:512], in_=BANKS[qb * 2])
            p.op("vector", "tensor_copy", waits=(("T", t2),),
                 out=bc[:, 512:1024], in_=BANKS[qb * 2 + 1])
            for ch in range(8):
                sl = slice(ch * T + qb * QB, ch * T + (qb + 1) * QB)
                if mode == "outer":
                    p.op("vector", "tensor_sub", out=u32[:],
                         in0=xa[:, sl], in1=bc[:, 0:512])
                    p.op("vector", "tensor_mul", out=xa[:, sl],
                         in0=u32[:], in1=bc[:, 512:1024])
                else:
                    p.op("vector", "tensor_sub",
                         out=xh[:, qb * QB:(qb + 1) * QB],
                         in0=xa[:, sl], in1=bc[:, 0:512])
                    p.op("vector", "scalar_tensor_tensor",
                         out=sqc[:, qb * QB:(qb + 1) * QB],
                         in0=xh[:, qb * QB:(qb + 1) * QB],
                         scalar=col(l, gname, ch), in1=bc[:, 512:1024],
                         op0=ALU.mult, op1=ALU.mult)
                    p.op("vector", "tensor_scalar_add",
                         out=dst_bf[:, sl], in0=sqc[:, qb * QB:(qb + 1) * QB],
                         scalar1=col(l, bname, ch))
        vend = p.cnt["V"]
        if mode == "outer":
            for ch in range(8):
                p.op("vector", "reduce_max", out=sc[:, 8 + ch:9 + ch],
                     in_=xa[:, ch * T:(ch + 1) * T], axis=AX,
                     apply_absolute_value=True)
            def gap():
                p.op("vector", "tensor_copy", out=sc[:, 7:8],
                     in_=identf[:, 1:2])
            p.op("vector", "tensor_max", out=sc[:, 1:2],
                 in0=sc[:, 8:9], in1=sc[:, 9:10])
            p.op("vector", "tensor_max", out=sc[:, 2:3],
                 in0=sc[:, 10:11], in1=sc[:, 11:12])
            p.op("vector", "tensor_max", out=sc[:, 3:4],
                 in0=sc[:, 12:13], in1=sc[:, 13:14])
            p.op("vector", "tensor_max", out=sc[:, 4:5],
                 in0=sc[:, 14:15], in1=sc[:, 15:16])
            p.op("vector", "tensor_max", out=sc[:, 5:6],
                 in0=sc[:, 1:2], in1=sc[:, 2:3])
            gap()
            p.op("vector", "tensor_max", out=sc[:, 6:7],
                 in0=sc[:, 3:4], in1=sc[:, 4:5])
            gap()
            gap()
            vend = p.op("vector", "tensor_max", out=sc[:, 0:1],
                        in0=sc[:, 5:6], in1=sc[:, 6:7])
        return vend

    def emit_colmax(src_col, dstrow, waits=()):
        tr = p.op("tensor", "transpose", waits=waits,
                  out=pD[0:1, 0:128], in_=src_col, identity=identf[:])
        return p.op("vector", "reduce_max", waits=(("T", tr),),
                    out=dstrow, in_=pD[0:1, 0:128], axis=AX)

    def emit_bcast_col(srcrow, dst_col, waits=()):
        t = mm(pD[:, 0:1], onesf1[0:1, :], srcrow, waits=waits)
        return p.op("scalar", "copy", waits=(("T", t),),
                    out=dst_col, in_=pD[:, 0:1])

    for ch in range(8):
        p.dma("sync", xa[:, ch * T:(ch + 1) * T], xT[128 * ch:128 * (ch + 1), :])
    LX = p.cnt["L"]
    for e in ("vector", "tensor", "scalar"):
        p.wait(e, (("L", LX),))

    reset_scratch()
    vmax = emit_ln("outer")
    vred = emit_colmax(sc[:, 0:1], arst[0:1, 0:1], waits=(("V", vmax),))
    v6 = p.op("vector", "memset", ap=arst[0:1, 1:16], constant=0.0)
    g1 = p.dma("gpsimd", ar_in[0], arst[0:1, :], waits=(("V", max(vred, v6)),))
    p.op("gpsimd", "collective_compute", waits=(("G", g1),), inc="C",
         kind="AllReduce", op=ALU.max, replica_groups=[list(range(NCORE))],
         ins=[ar_in[0]], outs=[ar_out[0]])
    CAR1 = p.cnt["C"]
    p.wait("sync", (("C", CAR1),))
    p.wait("gpsimd", (("C", CAR1),))

    if mkdbg("xn", [D, T]) is not None:
        for ch in range(8):
            p.dma("gpsimd", dbg_out["xn"][128 * ch:128 * (ch + 1), :],
                  xa[:, ch * T:(ch + 1) * T], waits=(("V", vmax),))

    p.fence()

    def emit_wquant():
        reset_scratch()
        wst = sb("wq_st", [128, 2816], F32)
        wq2 = sb("wq_f2", [128, 2816], F32)
        i8b = sb("wq_i8", [128, 2816], I8)
        bfb = sb("wq_bf", [128, 2816], BF16)
        tst = sb("wq_t", [128, 128], BF16)
        rmx = sb("wq_rmx", [128, 4], F32)
        vfree = p.cnt["V"]
        gfree = p.cnt["G"]
        pT = pD[:, 512:1024].bitcast(BF16)
        for wsrc, wdst, nch, wid, swap in ((wg2, wgt, NIC, D, False),
                                           (wu2, wut, NIC, D, False),
                                           (wd2, wdt, 8, DF, True)):
            for chh in range(nch):
                ld = p.dma("sync", wst[:, 0:wid],
                           wsrc[128 * chh:128 * (chh + 1), :],
                           waits=(("V", vfree),))
                vr = p.op("vector", "reduce_max", waits=(("L", ld),),
                          out=rmx[:, 0:1], in_=wst[:, 0:wid], axis=AX,
                          apply_absolute_value=True)
                s1 = p.op("scalar", "activation", waits=(("V", vr),),
                          out=rmx[:, 1:2], in_=rmx[:, 0:1], func=AF.Copy,
                          scale=1.0 / 127.0)
                p.op("vector", "reciprocal", waits=(("S", s1),),
                     out=rmx[:, 2:3], in_=rmx[:, 1:2])
                p.op("vector", "tensor_copy", out=rmx[:, 3:4], in_=rmx[:, 1:2])
                p.op("vector", "tensor_copy", out=rmx[:, 3:4], in_=rmx[:, 1:2])
                p.op("vector", "tensor_scalar_mul", out=wq2[:, 0:wid],
                     in0=wst[:, 0:wid], scalar1=rmx[:, 2:3])
                p.op("vector", "tensor_copy", out=i8b[:, 0:wid],
                     in_=wq2[:, 0:wid])
                vfree = p.op("vector", "tensor_copy", out=wq2[:, 0:wid],
                             in_=i8b[:, 0:wid])
                vbf = p.op("vector", "tensor_scalar_mul", out=bfb[:, 0:wid],
                           in0=wq2[:, 0:wid], scalar1=rmx[:, 1:2])
                if chh == 0 and wsrc is wg2 and mkdbg("wq0", [128, 8]) is not None:
                    p.dma("gpsimd", dbg_out["wq0"][:, 0:4], rmx[:, 0:4],
                          waits=(("V", vbf),))
                if chh == 0 and wsrc is wg2 and mkdbg("bfb0", [128, 1024], BF16) is not None:
                    p.dma("gpsimd", dbg_out["bfb0"], bfb[:, 0:1024],
                          waits=(("V", vbf),))
                for j in range(wid // 128):
                    tt = p.op("tensor", "transpose", out=pT[:, 0:128],
                              in_=bfb[:, 128 * j:128 * (j + 1)],
                              identity=identb[:],
                              waits=(("V", vbf), ("G", gfree)))
                    vt = p.op("vector", "tensor_copy",
                              waits=(("T", tt), ("G", gfree)),
                              out=tst[:], in_=pT[:, 0:128])
                    dstap = (wdst[j, :, 128 * chh:128 * (chh + 1)] if swap
                             else wdst[chh, :, 128 * j:128 * (j + 1)])
                    gfree = p.dma("gpsimd", dstap, tst[:], waits=(("V", vt),))
        return gfree

    GWQ = emit_wquant() if stage >= 1 else p.cnt["G"]
    p.fence()

    def emit_xq_gateup():
        reset_scratch()
        i8b = sb("xq_i8", [128, T], I8)
        qf = sb("xq_f2", [128, T], F32)
        gst = sb("gu_g", [128, 1024], BF16)
        ust = sb("gu_u", [128, 1024], BF16)
        gsf = sb("gu_sil", [128, QB], F32)
        hbf = sb("gu_h", [128, QB], F32)
        lcl = p.dma("sync", arst[0:1, 0:2], clipin)
        gld = p.dma("gpsimd", rows[0:1, 0:16], ar_out[0], waits=(("C", CAR1),))
        vg = p.op("vector", "reduce_max", waits=(("G", gld),),
                  out=scr[0:1, 32:33], in_=rows[0:1, 0:16], axis=AX)
        sg = p.op("scalar", "activation", waits=(("L", lcl),),
                  out=scr[0:1, 33:34], in_=arst[0:1, 0:1], func=AF.Sigmoid)
        p.op("vector", "tensor_mul", waits=(("S", sg), ("V", vg)), sep=True,
             out=scr[0:1, 34:35], in0=scr[0:1, 32:33], in1=scr[0:1, 33:34])
        p.op("vector", "tensor_scalar_mul", out=scr[0:1, 34:35], sep=True,
             in0=scr[0:1, 34:35], scalar1=1.0 / 127.0)
        v3 = p.op("vector", "reciprocal", out=scr[0:1, 35:36], sep=True,
                  in_=scr[0:1, 34:35])
        s2 = emit_bcast_col(scr[0:1, 34:35], sc[:, 1:2], waits=(("V", v3),))
        s3 = emit_bcast_col(scr[0:1, 35:36], sc[:, 2:3], waits=(("S", s2),))
        if mkdbg("qsc", [1, 64]) is not None:
            p.dma("gpsimd", dbg_out["qsc"], scr[0:1, :], waits=(("S", s3),))
        vq = 0
        for ch in range(8):
            p.op("vector", "tensor_scalar_mul", waits=(("S", s3),),
                 out=qf[:], in0=xa[:, ch * T:(ch + 1) * T], scalar1=sc[:, 2:3])
            p.op("vector", "tensor_copy", out=i8b[:], in_=qf[:])
            p.op("vector", "tensor_copy", out=qf[:], in_=i8b[:])
            vq = p.op("vector", "tensor_scalar_mul",
                      out=da[:, ch * T:(ch + 1) * T],
                      in0=qf[:], scalar1=sc[:, 1:2])
        if mkdbg("xq", [D, T], BF16) is not None:
            for ch in range(8):
                p.dma("gpsimd", dbg_out["xq"][128 * ch:128 * (ch + 1), :],
                      da[:, ch * T:(ch + 1) * T], waits=(("V", vq),))
        p.op("vector", "memset", ap=sc[:, 0:1], constant=0.0)
        p.op("vector", "memset", ap=sc[:, 14:15], constant=0.0)
        vhm = p.op("vector", "memset", ap=sc[:, 15:16], constant=0.0)
        def gap():
            p.op("vector", "tensor_copy", out=sc[:, 13:14], in_=identf[:, 1:2])
        gfree = p.cnt["G"]
        vfree_g = p.cnt["V"]
        tprev = p.cnt["T"]
        for it in range(NIC):
            ldg = p.dma("sync", gst[:], wgt[it, :, :],
                        waits=(("T", tprev), ("G", GWQ)))
            ldu = p.dma("sync", ust[:], wut[it, :, :], waits=())
            for qb in range(NQB):
                idx = it * NQB + qb
                cell = 14 + (idx % 2)
                for h in range(8):
                    mm(pA[:, 0:512], gst[:, 128 * h:128 * (h + 1)],
                       da[:, h * T + qb * QB:h * T + (qb + 1) * QB],
                       start=(h == 0), stop=(h == 7),
                       waits=(("L", ldu), ("V", max(vq, vhm))) if h == 0 else ())
                tu = None
                for h in range(8):
                    tu = mm(pA[:, 512:1024], ust[:, 128 * h:128 * (h + 1)],
                            da[:, h * T + qb * QB:h * T + (qb + 1) * QB],
                            start=(h == 0), stop=(h == 7))
                tprev = tu
                ss = p.op("scalar", "activation", waits=(("T", tu), ("V", vfree_g)),
                          out=gsf[:], in_=pA[:, 0:512], func=AF.Silu)
                vmul = p.op("vector", "tensor_mul", waits=(("S", ss), ("G", gfree)),
                            out=hbf[:], in0=gsf[:], in1=pA[:, 512:1024])
                p.op("vector", "tensor_max", out=sc[:, 0:1],
                     in0=sc[:, 0:1], in1=sc[:, cell:cell + 1])
                vfree_g = p.op("vector", "reduce_max", out=sc[:, cell:cell + 1],
                               in_=hbf[:], axis=AX, apply_absolute_value=True)
                gfree = p.dma("gpsimd", h_scr[128 * it:128 * (it + 1),
                                              qb * QB:(qb + 1) * QB],
                              hbf[:], waits=(("V", vmul),))
                p.wait("tensor", (("V", vfree_g),))
        gap()
        gap()
        p.op("vector", "tensor_max", out=sc[:, 0:1],
             in0=sc[:, 0:1], in1=sc[:, 14:15])
        gap()
        gap()
        vlast = p.op("vector", "tensor_max", out=sc[:, 0:1],
                     in0=sc[:, 0:1], in1=sc[:, 15:16])
        vr = emit_colmax(sc[:, 0:1], arst[0:1, 0:1], waits=(("V", vlast),))
        v7 = p.op("vector", "memset", ap=arst[0:1, 1:16], constant=0.0)
        g2 = p.dma("gpsimd", ar_in[1], arst[0:1, :],
                   waits=(("V", max(vr, v7)),))
        p.op("gpsimd", "collective_compute", waits=(("G", g2),), inc="C",
             kind="AllReduce", op=ALU.max, replica_groups=[list(range(NCORE))],
             ins=[ar_in[1]], outs=[ar_out[1]])
        p.wait("sync", (("C", p.cnt["C"]),))
        p.wait("gpsimd", (("C", p.cnt["C"]),))

    if stage >= 2:
        emit_xq_gateup()
    CAR2 = p.cnt["C"]
    p.fence()

    def emit_qkv(l, xlb):
        wbuf = sb("qk_w", [128, 8192], BF16)
        wstg = sb("qk_ws", [128, 1024], F32)
        stg = sb("qk_stg", [128, QB], BF16)
        sqs = sb("qk_sq", [128, QB], BF16)
        rot = sb("qk_rot", [128, T], F32)
        lrot = p.dma("sync", rot[:], rotd)
        wsrc = {"q": wq3, "k": wk3, "v": wv3}
        vwcast = p.cnt["V"]
        slast = p.cnt["V"]
        vmaxdone = p.cnt["V"]
        glast = p.cnt["G"]
        vstg = p.cnt["V"]
        for which in ("k", "v", "q"):
            for h in range(8):
                ld = p.dma("sync", wstg[:],
                           wsrc[which][l, 128 * h:128 * (h + 1), :],
                           waits=(("V", vwcast), ("T", p.cnt["T"])))
                vwcast = p.op("vector", "tensor_copy", waits=(("L", ld),),
                              out=wbuf[:, 1024 * h:1024 * (h + 1)], in_=wstg[:])
            bname = {"q": "bq", "k": "bk", "v": "bv"}[which]
            for m in range(8):
                for qb in range(NQB):
                    bank = BANKS[(m * NQB + qb) % 2]
                    tl = None
                    for h in range(8):
                        tl = mm(bank,
                                wbuf[:, 1024 * h + 128 * m:1024 * h + 128 * (m + 1)],
                                xlb[:, h * T + qb * QB:h * T + (qb + 1) * QB],
                                start=(h == 0), stop=(h == 7),
                                waits=(("V", max(slast, vwcast)),) if h == 0 else ())
                    if which == "v":
                        slast = p.op("vector", "tensor_scalar_add", inc="V",
                                     waits=(("T", tl), ("G", glast)),
                                     out=stg[:], in0=bank,
                                     scalar1=col(l, bname, m))
                        glast = p.dma("gpsimd",
                                      ag_in[l][D + 128 * m:D + 128 * (m + 1),
                                               qb * QB:(qb + 1) * QB],
                                      stg[:], waits=(("V", slast),))
                    else:
                        slast = p.op("vector", "tensor_scalar_add", inc="V",
                                     waits=(("T", tl), ("G", glast),
                                            ("L", lrot), ("V", vstg)),
                                     out=stg[:], in0=bank,
                                     scalar1=col(l, bname, m))
                        vr = p.op("vector", "tensor_mul",
                                  out=stg[:], in0=stg[:],
                                  in1=rot[:, qb * QB:(qb + 1) * QB])
                        if which == "q":
                            vstg = p.op("vector", "tensor_copy",
                                        out=qt[:, m * T + qb * QB:m * T + (qb + 1) * QB],
                                        in_=stg[:])
                        else:
                            glast = p.dma("gpsimd",
                                          ag_in[l][128 * m:128 * (m + 1),
                                                   qb * QB:(qb + 1) * QB],
                                          stg[:], waits=(("V", vr),))
                            vstg = vr
                        vsq = p.op("vector", "tensor_mul", out=sqs[:],
                                   in0=stg[:], in1=stg[:])
                        tn = mm(BANKS[2][0:1, :], onesb[:], sqs[:],
                                waits=(("V", vsq),))
                        p.op("vector", "reduce_max", waits=(("T", tn),),
                             out=scr[0:1, 36:37], in_=BANKS[2][0:1, :], axis=AX)
                        dsti = (0 if which == "k" else 8) + m
                        dst = scr[0:1, dsti:dsti + 1]
                        if qb == 0:
                            vmaxdone = p.op("vector", "tensor_copy", out=dst,
                                            sep=True, in_=scr[0:1, 36:37])
                        else:
                            vmaxdone = p.op("vector", "tensor_max", out=dst,
                                            sep=True, in0=dst, in1=scr[0:1, 36:37])
                        vstg = max(vstg, vmaxdone)
        return vmaxdone, glast

    def emit_ag(l, vnorm, glast):
        p.op("vector", "memset", waits=(("V", vnorm),), inc="V",
             ap=nrow[0:1, :], constant=0.0)
        vb = p.op("vector", "tensor_copy", sep=True,
                  out=nrow[0:1, 0:8], in_=scr[0:1, 0:8])
        g3 = p.dma("gpsimd", ag_in[l][2 * D:2 * D + 1, :], nrow[0:1, :],
                   waits=(("V", vb), ("G", glast)))
        p.op("gpsimd", "collective_compute", waits=(("G", g3),), inc="C",
             kind="AllGather", op=ALU.bypass,
             replica_groups=[[0, 1, 2, 3], [4, 5, 6, 7]],
             ins=[ag_in[l]], outs=[ag_out[l]])
        cag = p.cnt["C"]
        p.op("gpsimd", "collective_compute", waits=(("C", cag),), inc="C",
             kind="AllReduce", op=ALU.max, replica_groups=[list(range(NCORE))],
             ins=[bar_in[l]], outs=[bar_out[l]])
        cbar = p.cnt["C"]
        p.wait("sync", (("C", cbar),))
        p.wait("gpsimd", (("C", cbar),))
        return cbar

    def emit_w12cast(l):
        reset_scratch()
        wstg = sb("wc_st", [128, DI], F32)
        wbf = sb("wc_bf", [128, DI], BF16)
        vfree = p.cnt["V"]
        gfree = p.cnt["G"]
        for h in range(8):
            ld = p.dma("sync", wstg[:], w13[l, 128 * h:128 * (h + 1), :],
                       waits=(("V", vfree),))
            vfree = p.op("vector", "tensor_copy", waits=(("L", ld), ("G", gfree)),
                         out=wbf[:], in_=wstg[:])
            gfree = p.dma("gpsimd", w1s[128 * h:128 * (h + 1), :], wbf[:],
                          waits=(("V", vfree),))
        for ic in range(NI1):
            ld = p.dma("sync", wstg[:, 0:1024], w23[l, 128 * ic:128 * (ic + 1), :],
                       waits=(("V", vfree),))
            vfree = p.op("vector", "tensor_copy", waits=(("L", ld), ("G", gfree)),
                         out=wbf[:, 0:1024], in_=wstg[:, 0:1024])
            gfree = p.dma("gpsimd", w2s[128 * ic:128 * (ic + 1), :],
                          wbf[:, 0:1024], waits=(("V", vfree),))
        return gfree

    def emit_attention(l, cag):
        reset_scratch()
        kt = sb("at_k", [128, 8192], BF16)
        dnb = sb("at_dnb", [128, 512], F32)
        vtt = sb("at_vt", [128, T], BF16)
        vt = sb("at_v", [128, 64 * 130], BF16)
        pbuf = sb("at_p", [128, 2048], BF16)
        p.wait("sync", (("C", cag),))
        p.wait("gpsimd", (("C", cag),))
        gn = 0
        for r in range(4):
            gn = p.dma("gpsimd", nrm4[0:1, 8 * r:8 * (r + 1)],
                       ag_out[l][AGR * r + 2 * D:AGR * r + 2 * D + 1, 0:8])
        p.op("vector", "tensor_copy", waits=(("G", gn),),
             out=rows[0:1, 0:32], in_=nrm4[0:1, 0:32])
        vkm = p.op("vector", "reduce_max", sep=True, out=scr[0:1, 0:8],
                   in_=rows[0:1, 0:32].rearrange("a (r j) -> a j r", r=4),
                   axis=AX)
        v1 = p.op("vector", "tensor_mul", waits=(("V", vkm),), sep=True,
                  out=scr[0:1, 16:24], in0=scr[0:1, 0:8], in1=scr[0:1, 8:16])
        s1 = p.op("scalar", "activation", waits=(("V", v1),),
                  out=scr[0:1, 24:32], in_=scr[0:1, 16:24], func=AF.Sqrt)
        v2 = p.op("vector", "tensor_scalar_mul", waits=(("S", s1),), sep=True,
                  out=scr[0:1, 16:24], in0=scr[0:1, 24:32],
                  scalar1=-1.02 / 8.0)
        sprev = p.cnt["S"]
        for pp in range(8):
            sprev = emit_bcast_col(scr[0:1, 16 + pp:17 + pp],
                                   sc[:, 5 + pp:6 + pp],
                                   waits=(("V", v2), ("S", sprev)))
        pT = pD[:, 512:1024].bitcast(BF16)
        texp_prev = [p.cnt["S"]] * 2
        tpv_prev = [p.cnt["T"]] * 2
        vep_done = p.cnt["V"]
        sexp_last = p.cnt["S"]
        klast = 0
        for pp in range(8):
            for r in range(4):
                klast = p.dma("sync", kt[:, 2048 * r:2048 * (r + 1)],
                              ag_out[l][AGR * r + 128 * pp:AGR * r + 128 * (pp + 1), :],
                              waits=(("S", sexp_last),) if r == 0 else ())
            vtrdone = p.op("vector", "memset",
                           ap=vt[:].rearrange("p (c g) -> p c g", g=65)[:, :, 64:65],
                           constant=1.0, waits=(("T", tpv_prev[0]), ("T", tpv_prev[1])))
            for r in range(4):
                lv = p.dma("sync", vtt[:],
                           ag_out[l][AGR * r + D + 128 * pp:AGR * r + D + 128 * (pp + 1), :],
                           waits=(("V", vtrdone),))
                for j in range(16):
                    c = 16 * r + j
                    tt = p.op("tensor", "transpose", out=pT[:, 0:128],
                              in_=vtt[:, 128 * j:128 * (j + 1)],
                              identity=identb[:],
                              waits=(("L", lv), ("V", vtrdone)))
                    vtrdone = p.op("vector", "tensor_copy", waits=(("T", tt),),
                                   out=vt[:, 130 * c:130 * c + 130]
                                   .rearrange("p (g x) -> p g x", g=2)[:, :, 0:64],
                                   in_=pT[:, 0:128]
                                   .rearrange("p (g x) -> p g x", g=2))
            for qb in range(NQB):
                tpv = tpv_prev[1]
                for c in range(NKC):
                    par = c % 2
                    psc = pA if par == 0 else pB
                    mm(psc[:, 0:512], kt[0:64, 128 * c:128 * (c + 1)],
                       qt[0:64, pp * T + qb * QB:pp * T + (qb + 1) * QB],
                       waits=(("S", texp_prev[par]), ("L", klast),
                              ("V", vtrdone)))
                    tsc = mm(psc[:, 512:1024], kt[64:128, 128 * c:128 * (c + 1)],
                             qt[64:128, pp * T + qb * QB:pp * T + (qb + 1) * QB],
                             tp=(64, 0))
                    se = p.op("scalar", "activation",
                              waits=(("T", max(tsc, tpv_prev[par])),),
                              out=pbuf[:, 1024 * par:1024 * (par + 1)],
                              in_=psc[:, 0:1024], func=AF.Exp,
                              scale=0.125, bias=sc[:, 5 + pp:6 + pp])
                    texp_prev[par] = se
                    sexp_last = se
                    w0 = [("S", se)]
                    if c == 0:
                        w0.append(("V", vep_done))
                    mm(pC[0:65, 0:512], vt[:, 130 * c:130 * c + 65],
                       pbuf[:, 1024 * par:1024 * par + 512],
                       start=(c == 0), stop=(c == NKC - 1), waits=tuple(w0))
                    tpv = mm(pC[0:65, 512:1024], vt[:, 130 * c + 65:130 * c + 130],
                             pbuf[:, 1024 * par + 512:1024 * (par + 1)],
                             start=(c == 0), stop=(c == NKC - 1))
                    tpv_prev[par] = tpv
                v3 = p.op("vector", "reciprocal", waits=(("T", tpv),),
                          out=rows[0:1, 0:512], in_=pC[64:65, 0:512])
                t4 = mm(pD[0:64, 0:512], onesf1[0:1, 0:64], rows[0:1, 0:512],
                        waits=(("V", v3),))
                vb1 = p.op("vector", "tensor_copy", waits=(("T", t4),),
                           out=dnb[0:64, :], in_=pD[0:64, 0:512])
                va = p.op("vector", "tensor_mul",
                          out=da[0:64, pp * T + qb * QB:pp * T + (qb + 1) * QB],
                          in0=pC[0:64, 0:512], in1=dnb[0:64, :])
                v4 = p.op("vector", "reciprocal", sep=True,
                          out=rows[0:1, 512:1024], in_=pC[64:65, 512:1024])
                t5 = mm(pD[0:64, 0:512], onesf1[0:1, 0:64], rows[0:1, 512:1024],
                        waits=(("V", v4),))
                vb2 = p.op("vector", "tensor_copy", waits=(("T", t5),),
                           out=dnb[0:64, :], in_=pD[0:64, 0:512])
                vep_done = p.op("vector", "tensor_mul",
                                out=da[64:128, pp * T + qb * QB:pp * T + (qb + 1) * QB],
                                in0=pC[0:64, 512:1024], in1=dnb[0:64, :])
        return vep_done

    def emit_wo(l, vatt):
        if l == 0 and mkdbg("att0", [D, T], BF16) is not None:
            for ch in range(8):
                p.dma("gpsimd", dbg_out["att0"][128 * ch:128 * (ch + 1), :],
                      da[:, ch * T:(ch + 1) * T], waits=(("V", vatt),))
        reset_scratch()
        wbuf = sb("wo_w", [128, 8192], BF16)
        wstg = sb("wo_ws", [128, 1024], F32)
        vw = p.cnt["V"]
        for h in range(8):
            ld = p.dma("sync", wstg[:], wo3[l, 128 * h:128 * (h + 1), :],
                       waits=(("V", vw),))
            vw = p.op("vector", "tensor_copy", waits=(("L", ld),),
                      out=wbuf[:, 1024 * h:1024 * (h + 1)], in_=wstg[:])
        vv = [max(vw, vatt)] * 4
        for m in range(8):
            for qb in range(NQB):
                bi = (m * NQB + qb) % 4
                bank = BANKS[bi]
                tl = None
                for h in range(8):
                    tl = mm(bank,
                            wbuf[:, 1024 * h + 128 * m:1024 * h + 128 * (m + 1)],
                            da[:, h * T + qb * QB:h * T + (qb + 1) * QB],
                            start=(h == 0), stop=(h == 7),
                            waits=(("V", vv[bi]),) if h == 0 else ())
                vv[bi] = p.op("vector", "tensor_add", waits=(("T", tl),),
                              out=xa[:, m * T + qb * QB:m * T + (qb + 1) * QB],
                              in0=xa[:, m * T + qb * QB:m * T + (qb + 1) * QB],
                              in1=bank)
        return max(vv)

    def emit_ffn(l):
        if l == 1 and mkdbg("xm1", [D, T]) is not None:
            for ch in range(8):
                p.dma("gpsimd", dbg_out["xm1"][128 * ch:128 * (ch + 1), :],
                      xa[:, ch * T:(ch + 1) * T])
        if l == 0 and mkdbg("xm0", [D, T]) is not None:
            for ch in range(8):
                p.dma("gpsimd", dbg_out["xm0"][128 * ch:128 * (ch + 1), :],
                      xa[:, ch * T:(ch + 1) * T])
        reset_scratch()
        emit_ln("affine", l, "ln2_g", "ln2_b", dst_bf=qt)
        vln = p.cnt["V"]
        reset_scratch()
        w1st = sb("f_w1", [128, 4096], BF16)
        w2st = sb("f_w2", [128, 1024], BF16)
        vhid = p.cnt["V"]
        shid = p.cnt["S"]
        for qb in range(NQB):
            for mg in range(8):
                lw = 0
                for h in range(8):
                    lw = p.dma("sync", w1st[:, 512 * h:512 * (h + 1)],
                               w1s[128 * h:128 * (h + 1),
                                   512 * mg:512 * (mg + 1)],
                               waits=(("T", p.cnt["T"]), ("V", vln)) if h == 0 else ())
                for mt_in in range(4):
                    mt = mg * 4 + mt_in
                    bank = BANKS[mt % 2]
                    tl = None
                    for h in range(8):
                        w1w = (("L", lw), ("S", shid))
                        if mg == 0 and mt_in == 0:
                            w1w = w1w + (("V", vhid),)
                        tl = mm(bank,
                                w1st[:, 512 * h + 128 * mt_in:512 * h + 128 * (mt_in + 1)],
                                qt[:, h * T + qb * QB:h * T + (qb + 1) * QB],
                                start=(h == 0), stop=(h == 7),
                                waits=w1w if h == 0 else ())
                    shid = p.op("scalar", "activation", waits=(("T", tl),),
                                out=da[:, 512 * mt:512 * (mt + 1)], in_=bank,
                                func=AF.Gelu_apprx_sigmoid,
                                bias=col(l, "b1", mt), scale=1.0)
            if l == 0 and qb == 0 and mkdbg("hid0", [128, 8 * T], BF16) is not None:
                p.dma("gpsimd", dbg_out["hid0"], da[:], waits=(("S", shid),))
            tl2 = [None] * 8
            for ic in range(NI1):
                lw2 = p.dma("sync", w2st[:], w2s[128 * ic:128 * (ic + 1), :],
                            waits=(("T", tl2[7]),) if ic >= 1 else
                            (("T", p.cnt["T"]), ("V", vln)))
                for mo in range(8):
                    tl2[mo] = mm(BANKS[mo], w2st[:, 128 * mo:128 * (mo + 1)],
                                 da[:, 512 * ic:512 * (ic + 1)],
                                 start=(ic == 0), stop=(ic == NI1 - 1),
                                 waits=(("L", lw2), ("S", shid),
                                        ("V", vhid)) if mo == 0 else
                                 (("L", lw2),) if mo == 0 else ())
            for mo in range(8):
                vhid = p.op("vector", "tensor_add", waits=(("T", tl2[mo]),),
                            out=xa[:, mo * T + qb * QB:mo * T + (qb + 1) * QB],
                            in0=xa[:, mo * T + qb * QB:mo * T + (qb + 1) * QB],
                            in1=BANKS[mo])
        return vhid

    def emit_down():
        p.fence()
        reset_scratch()
        hst = sb("d_h", [128, QB], F32)
        hf2 = sb("d_f2", [128, QB], F32)
        ost = sb("d_o", [128, QB], F32)
        i8b = sb("d_i8", [128, QB], I8)
        hqb2 = [sb("d_hq0", [128, QB], BF16), sb("d_hq1", [128, QB], BF16)]
        wdst = sb("d_wd", [128, 1024], BF16)
        gld = p.dma("gpsimd", rows[0:1, 0:16], ar_out[1], waits=(("C", CAR2),))
        vg = p.op("vector", "reduce_max", waits=(("G", gld),),
                  out=scr[0:1, 40:41], in_=rows[0:1, 0:16], axis=AX)
        lcl = p.dma("sync", arst[0:1, 0:2], clipin)
        sg = p.op("scalar", "activation", waits=(("L", lcl),),
                  out=scr[0:1, 41:42], in_=arst[0:1, 1:2], func=AF.Sigmoid)
        p.op("vector", "tensor_mul", waits=(("S", sg), ("V", vg)), sep=True,
             out=scr[0:1, 42:43], in0=scr[0:1, 40:41], in1=scr[0:1, 41:42])
        p.op("vector", "tensor_scalar_mul", out=scr[0:1, 42:43], sep=True,
             in0=scr[0:1, 42:43], scalar1=1.0 / 127.0)
        v3 = p.op("vector", "reciprocal", out=scr[0:1, 43:44], sep=True,
                  in_=scr[0:1, 42:43])
        sa_ = emit_bcast_col(scr[0:1, 42:43], sc[:, 3:4], waits=(("V", v3),))
        sb_ = emit_bcast_col(scr[0:1, 43:44], sc[:, 4:5], waits=(("S", sa_),))
        if mkdbg("dsc", [1, 64]) is not None:
            p.dma("gpsimd", dbg_out["dsc"], scr[0:1, :], waits=(("S", sb_),))
        vq = p.cnt["V"]
        gof = p.cnt["G"]
        t0 = p.cnt["T"]
        thist = [t0, t0]
        for qb in range(NQB):
            tl2 = [None] * 8
            for ic in range(NIC):
                k = qb * NIC + ic
                par = k % 2
                hqb = hqb2[par]
                lh = p.dma("sync", hst[:],
                           h_scr[128 * ic:128 * (ic + 1), qb * QB:(qb + 1) * QB],
                           waits=(("V", vq),))
                lwd = p.dma("sync", wdst[:], wdt[ic, :, :],
                            waits=(("T", tl2[7]),) if ic >= 1 else
                            (("T", p.cnt["T"]),))
                p.op("vector", "tensor_scalar_mul",
                     waits=(("L", lh), ("S", sb_), ("T", thist[par])),
                     out=hf2[:], in0=hst[:], scalar1=sc[:, 4:5])
                p.op("vector", "tensor_copy", out=i8b[:], in_=hf2[:])
                p.op("vector", "tensor_copy", out=hf2[:], in_=i8b[:])
                vq = p.op("vector", "tensor_scalar_mul", out=hqb[:],
                          in0=hf2[:], scalar1=sc[:, 3:4])
                if qb == 0 and ic == 0 and mkdbg("hq00", [128, QB], BF16) is not None:
                    p.dma("gpsimd", dbg_out["hq00"], hqb[:], waits=(("V", vq),))
                if mkdbg("hqall", [128, NQB * NIC * QB], BF16) is not None:
                    p.dma("gpsimd", dbg_out["hqall"][:, k * QB:(k + 1) * QB],
                          hqb[:], waits=(("V", vq),))
                for mo in range(8):
                    tl2[mo] = mm(BANKS[mo], wdst[:, 128 * mo:128 * (mo + 1)],
                                 hqb[:], start=(ic == 0), stop=(ic == NIC - 1),
                                 waits=(("L", lwd), ("V", vq)) if mo == 0 else ())
                thist[par] = tl2[7]
            for mo in range(8):
                vv = p.op("vector", "tensor_copy",
                          waits=(("T", tl2[mo]), ("G", gof)),
                          out=ost[:], in_=BANKS[mo])
                gof = p.dma("gpsimd",
                            ffn_scr[128 * mo:128 * (mo + 1), qb * QB:(qb + 1) * QB],
                            ost[:], waits=(("V", vv),))
                vq = max(vq, vv)
        return gof

    def emit_layer(l):
        if l == 1 and mkdbg("xl0", [D, T]) is not None:
            for ch in range(8):
                p.dma("gpsimd", dbg_out["xl0"][128 * ch:128 * (ch + 1), :],
                      xa[:, ch * T:(ch + 1) * T])
        reset_scratch()
        emit_ln("affine", l, "ln1_g", "ln1_b", dst_bf=da)
        reset_scratch()
        vn, gl = emit_qkv(l, da)
        cag = emit_ag(l, vn, gl)
        if l == 0:
            if stage >= 3:
                emit_w12cast(0)
        else:
            if stage >= 6:
                emit_down()
        p.fence()
        if stage >= (4 if l == 0 else 7):
            vatt = emit_attention(l, cag)
            emit_wo(l, vatt)
            p.fence()
        if stage >= (5 if l == 0 else 8):
            if l == 1:
                emit_w12cast(1)
                p.fence()
            emit_ffn(l)
            p.fence()

    if stage >= 3:
        emit_layer(0)
    if stage >= 6:
        emit_layer(1)

    reset_scratch()
    xst = sb("p2_x", [128, T], F32)
    fst = sb("p2_f", [128, T], F32)
    gout = 0
    vdone = p.cnt["V"]
    for ch in range(8):
        l1 = p.dma("sync", xst[:], xT[128 * ch:128 * (ch + 1), :],
                   waits=(("V", vdone), ("G", gout)))
        l2 = p.dma("sync", fst[:], ffn_scr[128 * ch:128 * (ch + 1), :],
                   waits=(("G", p.cnt["G"]),))
        p.op("vector", "tensor_add", waits=(("L", l2),),
             out=xst[:], in0=xst[:], in1=xa[:, ch * T:(ch + 1) * T])
        vdone = p.op("vector", "tensor_add", out=xst[:], in0=xst[:], in1=fst[:])
        gout = p.dma("gpsimd", oT[128 * ch:128 * (ch + 1), :], xst[:],
                     waits=(("V", vdone),))
    p.wait("sync", (("G", gout),))

    for nm in ("xa_f",):
        if nm in dbg:
            dd = mkdbg(nm, [D, T])
            for ch in range(8):
                p.dma("gpsimd", dd[128 * ch:128 * (ch + 1), :],
                      xa[:, ch * T:(ch + 1) * T])
    if "h_d" in dbg:
        dd = mkdbg("h_d", [DF, T])
        p.dma("gpsimd", dd, h_scr, waits=(("G", p.cnt["G"]),))
    if "wgt_d" in dbg:
        dd = mkdbg("wgt_d", [NIC, 128, 1024], BF16)
        p.dma("gpsimd", dd, wgt, waits=(("G", p.cnt["G"]),))
    if "wut_d" in dbg:
        dd = mkdbg("wut_d", [NIC, 128, 1024], BF16)
        p.dma("gpsimd", dd, wut, waits=(("G", p.cnt["G"]),))
    if "f_d" in dbg:
        dd = mkdbg("f_d", [D, T])
        p.dma("gpsimd", dd, ffn_scr)
    for nm in ("qt_d", "da_d"):
        if nm in dbg:
            dd = mkdbg(nm, [D, T], BF16)
            srcb = qt if nm == "qt_d" else da
            for ch in range(8):
                p.dma("gpsimd", dd[128 * ch:128 * (ch + 1), :],
                      srcb[:, ch * T:(ch + 1) * T])

    assert ar.size <= 188 * 1024, f"sbuf over budget: {ar.size}" 
    p.check()
    with nc.Block() as block:
        p.build(block)
    return nc, sorted(dbg_out.keys())


_cache = {}


def kernel(**inputs):
    x = np.asarray(inputs["L0_attn_x"])
    odt = x.dtype

    key = (STAGE, tuple(sorted(DBG)))
    if key not in _cache:
        _cache[key] = build_program(STAGE, DBG)
    nc, dbg_names = _cache[key]

    def f32(a):
        return np.ascontiguousarray(np.asarray(a, dtype=np.float32))

    cx = float(np.asarray(inputs["L0_ffn_clip_x"]).ravel()[0])
    cd = float(np.asarray(inputs["L0_ffn_clip_down"]).ravel()[0])
    shared = {
        "wq3": f32(inputs["L0_attn_wq"]), "wk3": f32(inputs["L0_attn_wk"]),
        "wv3": f32(inputs["L0_attn_wv"]), "wo3": f32(inputs["L0_attn_wo"]),
        "w13": f32(inputs["L0_attn_w1"]), "w23": f32(inputs["L0_attn_w2"]),
        "wg2": f32(inputs["L0_ffn_w_gate"]), "wu2": f32(inputs["L0_ffn_w_up"]),
        "wd2": f32(inputs["L0_ffn_w_down"]),
        "identbd": np.eye(128, dtype=ml_dtypes.bfloat16),
        "identfd": np.eye(128, dtype=np.float32),
        "onesbd": np.ones((128, 1), dtype=ml_dtypes.bfloat16),
        "onesf1d": np.ones((1, 128), dtype=np.float32),
        "clipin": np.array([[cx, cd]], dtype=np.float32),
    }
    bias_host = np.zeros((128, NBC), np.float32)
    src = {"ln1_g": f32(inputs["L0_attn_ln1_g"]), "ln1_b": f32(inputs["L0_attn_ln1_b"]),
           "ln2_g": f32(inputs["L0_attn_ln2_g"]), "ln2_b": f32(inputs["L0_attn_ln2_b"]),
           "bq": f32(inputs["L0_attn_bq"]), "bk": f32(inputs["L0_attn_bk"]),
           "bv": f32(inputs["L0_attn_bv"]), "b1": f32(inputs["L0_attn_b1"])}
    for (l, nm, c), cc in BIAS_COLS.items():
        bias_host[:, cc] = src[nm][l, 128 * c:128 * (c + 1)]
    shared["biasd"] = bias_host

    in_maps = []
    for ci in range(NCORE):
        bb, qi = ci // 4, ci % 4
        t0 = qi * T
        m = dict(shared)
        m["xT"] = np.ascontiguousarray(f32(x[bb, t0:t0 + T, :]).T)
        m["rotd"] = host_rot_table(t0)
        in_maps.append(m)

    res = run_bass_kernel_spmd(nc, in_maps, core_ids=list(range(NCORE)))

    out = np.empty((B, S, D), np.float32)
    for ci in range(NCORE):
        bb, qi = ci // 4, ci % 4
        out[bb, qi * T:(qi + 1) * T, :] = res.results[ci]["oT"].T
    kernel._dbg = [{k: res.results[ci].get("dbg_" + k) for k in dbg_names}
                   for ci in range(NCORE)]
    return out.astype(odt)


# version 06
# speedup vs baseline: 1.0829x; 1.0829x over previous
import math
import contextlib
import numpy as np
import ml_dtypes
import concourse.bass as bass
import concourse.mybir as mybir
from concourse.bass_utils import run_bass_kernel_spmd

F32 = mybir.dt.float32
BF16 = mybir.dt.bfloat16
I8 = mybir.dt.int8
AF = mybir.ActivationFunctionType
ALU = mybir.AluOpType
AX = mybir.AxisListType.X

B, S, D = 2, 8192, 1024
NCORE = 8
T = 2048
DPH, ROTD = 64, 32
DI, DF = 4096, 2816
QB = 512
NQB = T // QB
NKC = S // 128
NIC = DF // 128
NI1 = DI // 128
AGR = 2 * D + 1

STAGE = 9
DBG = ()


def host_rot_table(t0):
    dim_exp = ROTD // 2
    freqs = 2.0 ** np.linspace(0.0, math.log2(10.0 / 2.0), dim_exp)
    pos = (1.0 + np.arange(t0, t0 + T, dtype=np.float64))[:, None]
    rad = pos * freqs * math.pi
    sn, cs = np.sin(rad), np.cos(rad)
    snt, cst = np.tile(sn, (1, 2)), np.tile(cs, (1, 2))
    Ah = np.ones((T, DPH))
    for j in range(ROTD):
        Ah[:, j] = snt[:, j] - cst[:, j] if j % 2 == 0 else snt[:, j] + cst[:, j]
    tab = np.ones((128, T))
    tab[0:DPH, :] = Ah.T
    tab[DPH:2 * DPH, :] = Ah.T
    return np.ascontiguousarray(tab.astype(np.float32))


def bias_col_map():
    m, cc = {}, 0
    for l in range(2):
        for nm, nch in (("ln1_g", 8), ("ln1_b", 8), ("ln2_g", 8), ("ln2_b", 8),
                        ("bq", 8), ("bk", 8), ("bv", 8), ("b1", 32)):
            for c in range(nch):
                m[(l, nm, c)] = cc
                cc += 1
    return m, cc


BIAS_COLS, NBC = bias_col_map()


class P:
    ENGS = ("sync", "gpsimd", "tensor", "vector", "scalar")

    def __init__(self, nc, es):
        self.nc = nc
        self.steps = {e: [] for e in self.ENGS}
        self.cnt = {}
        self.sems = {}
        for s in ("L", "G", "C", "T", "V", "S", "F"):
            h = es.enter_context(nc.semaphore(s))
            self.sems[s] = h
            self.cnt[s] = 0
        self.nfence = 0

    def emit(self, eng, fn, waits=(), inc=None, amt=1):
        v = None
        if inc is not None:
            self.cnt[inc] += amt
            v = self.cnt[inc]
        self.steps[eng].append((fn, tuple(waits), inc, amt))
        return v

    def op(self, eng, meth, waits=(), inc="default", amt=1, sep=False, **kw):
        if inc == "default":
            inc = {"tensor": "T", "vector": "V", "scalar": "S",
                   "gpsimd": "G", "sync": "L"}[eng]
        if sep:
            self.steps[eng].append((lambda e: e.nop(), (), None, 0))

        def fn(e, meth=meth, kw=kw):
            return getattr(e, meth)(**kw)
        return self.emit(eng, fn, waits, inc, amt)

    def dma(self, eng, out, in_, waits=()):
        sem = "L" if eng == "sync" else "G"
        return self.emit(eng, lambda e, o=out, i=in_: e.dma_start(out=o, in_=i),
                         waits, sem, 16)

    def wait(self, eng, waits):
        self.steps[eng].append((None, tuple(waits), None, 0))

    def fence(self):
        self.nfence += 1
        for e in self.ENGS:
            self.cnt["F"] += 1
            self.steps[e].append(
                (lambda eng: eng.sem_inc(self.sems["F"], 1), (), None, 0))
        tgt = self.cnt["F"]
        cL, cG = self.cnt["L"], self.cnt["G"]
        for e in self.ENGS:
            self.wait(e, [("F", tgt), ("L", cL), ("G", cG)])

    def check(self):
        heads = {e: 0 for e in self.ENGS}
        counts = {s: 0 for s in self.cnt}
        progress = True
        while progress:
            progress = False
            for e in self.ENGS:
                while heads[e] < len(self.steps[e]):
                    fn, waits, inc, amt = self.steps[e][heads[e]]
                    if all(counts[s] >= v for s, v in waits):
                        if inc is not None:
                            counts[inc] += amt
                        elif fn is not None and amt == 0:
                            counts["F"] += 1
                        heads[e] += 1
                        progress = True
                    else:
                        break
        stuck = {e: heads[e] for e in self.ENGS if heads[e] < len(self.steps[e])}
        if stuck:
            det = {}
            for e in stuck:
                fn, waits, inc, amt = self.steps[e][heads[e]]
                det[e] = [(s, v, counts[s]) for s, v in waits if counts[s] < v]
            raise RuntimeError(f"schedule deadlock at {stuck}: {det}")

    def build(self, block):
        for ename in self.ENGS:
            sec = getattr(block, ename)
            steps = self.steps[ename]

            def body(eng, steps=steps):
                done = {}
                for fn, waits, inc, amt in steps:
                    ws = {}
                    for sname, v in waits:
                        if v > done.get(sname, 0):
                            ws[sname] = max(ws.get(sname, 0), v)
                    ws = sorted(ws.items())
                    while len(ws) > 2:
                        eng.wait_ge(self.sems[ws[0][0]], ws[0][1])
                        eng.wait_ge(self.sems[ws[1][0]], ws[1][1])
                        eng.nop()
                        for sname, v in ws[0:2]:
                            done[sname] = max(done.get(sname, 0), v)
                        ws = ws[2:]
                    nw = len(ws)
                    for sname, v in ws:
                        eng.wait_ge(self.sems[sname], v)
                        done[sname] = max(done.get(sname, 0), v)
                    if fn is None:
                        if nw:
                            eng.nop()
                        continue
                    if nw > 1:
                        eng.nop()
                    ins = fn(eng)
                    if inc is not None:
                        ins.then_inc(self.sems[inc], amt)
            sec(body)


def build_program(stage=STAGE, dbg=DBG):
    nc = bass.Bass("TRN2", target_bir_lowering=False, debug=False,
                   detect_race_conditions=False)

    xT = nc.dram_tensor("xT", [D, T], F32, kind="ExternalInput").ap()
    oT = nc.dram_tensor("oT", [D, T], F32, kind="ExternalOutput").ap()
    wq3 = nc.dram_tensor("wq3", [2, D, D], F32, kind="ExternalInput").ap()
    wk3 = nc.dram_tensor("wk3", [2, D, D], F32, kind="ExternalInput").ap()
    wv3 = nc.dram_tensor("wv3", [2, D, D], F32, kind="ExternalInput").ap()
    wo3 = nc.dram_tensor("wo3", [2, D, D], F32, kind="ExternalInput").ap()
    w13 = nc.dram_tensor("w13", [2, D, DI], F32, kind="ExternalInput").ap()
    w23 = nc.dram_tensor("w23", [2, DI, D], F32, kind="ExternalInput").ap()
    wg2 = nc.dram_tensor("wg2", [DF, D], F32, kind="ExternalInput").ap()
    wu2 = nc.dram_tensor("wu2", [DF, D], F32, kind="ExternalInput").ap()
    wd2 = nc.dram_tensor("wd2", [D, DF], F32, kind="ExternalInput").ap()
    biasd = nc.dram_tensor("biasd", [128, NBC], F32, kind="ExternalInput").ap()
    clipin = nc.dram_tensor("clipin", [1, 2], F32, kind="ExternalInput").ap()
    rotd = nc.dram_tensor("rotd", [128, T], F32, kind="ExternalInput").ap()
    identbd = nc.dram_tensor("identbd", [128, 128], BF16, kind="ExternalInput").ap()
    identfd = nc.dram_tensor("identfd", [128, 128], F32, kind="ExternalInput").ap()
    onesbd = nc.dram_tensor("onesbd", [128, 1], BF16, kind="ExternalInput").ap()
    onesf1d = nc.dram_tensor("onesf1d", [1, 128], F32, kind="ExternalInput").ap()

    dbg_out = {}
    def mkdbg(name, shape, dt=F32):
        if name in dbg and name not in dbg_out:
            dbg_out[name] = nc.dram_tensor("dbg_" + name, shape, dt,
                                           kind="ExternalOutput").ap()
        return dbg_out.get(name)

    ag_in = [nc.dram_tensor(f"ag_in{l}", [AGR, T], BF16).ap() for l in range(2)]
    ag_out = [nc.dram_tensor(f"ag_out{l}", [4 * AGR, T], BF16).ap() for l in range(2)]
    ar_in = [nc.dram_tensor(f"ar_in{i}", [1, 16], F32).ap() for i in range(2)]
    ar_out = [nc.dram_tensor(f"ar_out{i}", [1, 16], F32).ap() for i in range(2)]
    bar_in = [nc.dram_tensor(f"bar_in{i}", [1, 16], F32).ap() for i in range(2)]
    bar_out = [nc.dram_tensor(f"bar_out{i}", [1, 16], F32).ap() for i in range(2)]
    wgt = nc.dram_tensor("wgt", [NIC, 128, 1024], BF16).ap()
    wut = nc.dram_tensor("wut", [NIC, 128, 1024], BF16).ap()
    wdt = nc.dram_tensor("wdt", [NIC, 128, 1024], BF16).ap()
    w1s = nc.dram_tensor("w1s", [D, DI], BF16).ap()
    w2s = nc.dram_tensor("w2s", [DI, D], BF16).ap()
    h_scr = nc.dram_tensor("h_scr", [DF, T], F32).ap()
    ffn_scr = nc.dram_tensor("ffn_scr", [D, T], F32).ap()

    es = contextlib.ExitStack()
    p = P(nc, es)

    class Arena:
        def __init__(self, nc, size):
            self.nc = nc
            self.guard = es.enter_context(
                nc.sbuf_tensor("arena", [128, size], mybir.dt.uint8))
            self.base = nc.lookup_mloc(self.guard).addr
            self.addr = 0
            self.size = 0
            self.n = 0

        def sbuf_tensor(self, name, shape, dt):
            import functools
            import operator
            per = functools.reduce(operator.mul, shape[1:]) * mybir.dt.size(dt)
            a = (self.addr + 31) // 32 * 32
            self.n += 1
            h = self.nc.alloc_sbuf_tensor_at(
                f"{name}_{self.n}", shape, dt, offset=self.base + a)
            self.addr = a + per
            self.size = max(self.size, self.addr)
            return h

    ar = Arena(nc, 188 * 1024)
    def sb(name, shape, dt):
        return ar.sbuf_tensor(name, shape, dt)

    xa = sb("xa", [128, 8 * T], F32)
    qt = sb("qt", [128, 8 * T], BF16)
    da = sb("da", [128, 8 * T], BF16)
    bias_sb = sb("bias_sb", [128, NBC], F32)
    identb = sb("identb", [128, 128], BF16)
    identf = sb("identf", [128, 128], F32)
    onesb = sb("onesb", [128, 1], BF16)
    onesf1 = sb("onesf1", [1, 128], F32)
    rows = sb("rows", [1, 1536], F32)
    sc = sb("sc", [128, 16], F32)
    scr = sb("scr", [1, 64], F32)
    arst = sb("arst", [1, 16], F32)
    nrm4 = sb("nrm4", [1, 32], BF16)
    nrow = sb("nrow", [1, T], BF16)
    scratch_base = ar.addr

    def reset_scratch():
        ar.addr = scratch_base

    pes = contextlib.ExitStack()
    pA = pes.enter_context(nc.psum_tensor("pA", [128, 1024], F32)).ap()
    pB = pes.enter_context(nc.psum_tensor("pB", [128, 1024], F32)).ap()
    pC = pes.enter_context(nc.psum_tensor("pC", [128, 1024], F32)).ap()
    pD = pes.enter_context(nc.psum_tensor("pD", [128, 1024], F32)).ap()
    BANKS = [pA[:, 0:512], pA[:, 512:1024], pB[:, 0:512], pB[:, 512:1024],
             pC[:, 0:512], pC[:, 512:1024], pD[:, 0:512], pD[:, 512:1024]]

    def mm(out, lhsT, rhs, start=True, stop=True, waits=(), tp=None):
        kw = {"tile_position": tp} if tp is not None else {}
        return p.op("tensor", "matmul", waits=waits, out=out, lhsT=lhsT, rhs=rhs,
                    start=start, stop=stop, skip_group_check=True, **kw)

    def col(l, nm, c):
        i = BIAS_COLS[(l, nm, c)]
        return bias_sb[:, i:i + 1]

    p.dma("sync", bias_sb[:], biasd)
    p.dma("sync", identb[:], identbd)
    p.dma("sync", identf[:], identfd)
    p.dma("sync", onesb[:], onesbd)
    p.dma("sync", onesf1[:], onesf1d)

    def emit_ln(mode, l=None, gname=None, bname=None, dst_bf=None):
        xh = sb("ln_xh", [128, T], BF16)
        sqc = sb("ln_sq", [128, T], BF16)
        bc = sb("ln_bc", [128, 1024], F32)
        u32 = sb("ln_u", [128, QB], F32)
        last_stat = p.cnt["T"]
        for ch in range(8):
            w = (("T", last_stat),)
            vc = p.op("vector", "tensor_copy", waits=w, out=xh[:],
                      in_=xa[:, ch * T:(ch + 1) * T])
            vs = p.op("vector", "tensor_mul", out=sqc[:], in0=xh[:], in1=xh[:])
            for qb in range(NQB):
                mm(BANKS[qb * 2][0:1, :], onesb[:], xh[:, qb * QB:(qb + 1) * QB],
                   start=(ch == 0), stop=(ch == 7), waits=(("V", vc),))
                last_stat = mm(BANKS[qb * 2 + 1][0:1, :], onesb[:],
                               sqc[:, qb * QB:(qb + 1) * QB],
                               start=(ch == 0), stop=(ch == 7), waits=(("V", vs),))
        for qb in range(NQB):
            w = (("T", last_stat),)
            p.op("vector", "tensor_scalar_mul", waits=w,
                 out=rows[0:1, 0:512], in0=BANKS[qb * 2][0:1, :],
                 scalar1=1.0 / 1024.0)
            p.op("vector", "tensor_scalar_mul",
                 out=rows[0:1, 512:1024], in0=BANKS[qb * 2 + 1][0:1, :],
                 scalar1=1.0 / 1024.0)
            p.op("vector", "tensor_mul", out=bc[0:1, 0:512],
                 in0=rows[0:1, 0:512], in1=rows[0:1, 0:512])
            p.op("vector", "tensor_sub", out=rows[0:1, 512:1024],
                 in0=rows[0:1, 512:1024], in1=bc[0:1, 0:512])
            v3 = p.op("vector", "tensor_scalar_add", out=rows[0:1, 512:1024],
                      in0=rows[0:1, 512:1024], scalar1=1e-5)
            s1 = p.op("scalar", "activation", waits=(("V", v3),),
                      out=rows[0:1, 512:1024], in_=rows[0:1, 512:1024],
                      func=AF.Sqrt)
            v4 = p.op("vector", "reciprocal", waits=(("S", s1),),
                      out=rows[0:1, 1024:1536], in_=rows[0:1, 512:1024])
            t1 = mm(BANKS[qb * 2], onesf1[0:1, :], rows[0:1, 0:512],
                    waits=(("V", v4),))
            t2 = mm(BANKS[qb * 2 + 1], onesf1[0:1, :], rows[0:1, 1024:1536],
                    waits=(("V", v4),))
            p.op("vector", "tensor_copy", waits=(("T", t1),),
                 out=bc[:, 0:512], in_=BANKS[qb * 2])
            p.op("vector", "tensor_copy", waits=(("T", t2),),
                 out=bc[:, 512:1024], in_=BANKS[qb * 2 + 1])
            for ch in range(8):
                sl = slice(ch * T + qb * QB, ch * T + (qb + 1) * QB)
                if mode == "outer":
                    p.op("vector", "tensor_sub", out=u32[:],
                         in0=xa[:, sl], in1=bc[:, 0:512])
                    p.op("vector", "tensor_mul", out=xa[:, sl],
                         in0=u32[:], in1=bc[:, 512:1024])
                else:
                    p.op("vector", "tensor_sub",
                         out=xh[:, qb * QB:(qb + 1) * QB],
                         in0=xa[:, sl], in1=bc[:, 0:512])
                    p.op("vector", "scalar_tensor_tensor",
                         out=sqc[:, qb * QB:(qb + 1) * QB],
                         in0=xh[:, qb * QB:(qb + 1) * QB],
                         scalar=col(l, gname, ch), in1=bc[:, 512:1024],
                         op0=ALU.mult, op1=ALU.mult)
                    p.op("vector", "tensor_scalar_add",
                         out=dst_bf[:, sl], in0=sqc[:, qb * QB:(qb + 1) * QB],
                         scalar1=col(l, bname, ch))
        vend = p.cnt["V"]
        if mode == "outer":
            for ch in range(8):
                p.op("vector", "reduce_max", out=sc[:, 8 + ch:9 + ch],
                     in_=xa[:, ch * T:(ch + 1) * T], axis=AX,
                     apply_absolute_value=True)
            def gap():
                p.op("vector", "tensor_copy", out=sc[:, 7:8],
                     in_=identf[:, 1:2])
            p.op("vector", "tensor_max", out=sc[:, 1:2],
                 in0=sc[:, 8:9], in1=sc[:, 9:10])
            p.op("vector", "tensor_max", out=sc[:, 2:3],
                 in0=sc[:, 10:11], in1=sc[:, 11:12])
            p.op("vector", "tensor_max", out=sc[:, 3:4],
                 in0=sc[:, 12:13], in1=sc[:, 13:14])
            p.op("vector", "tensor_max", out=sc[:, 4:5],
                 in0=sc[:, 14:15], in1=sc[:, 15:16])
            p.op("vector", "tensor_max", out=sc[:, 5:6],
                 in0=sc[:, 1:2], in1=sc[:, 2:3])
            gap()
            p.op("vector", "tensor_max", out=sc[:, 6:7],
                 in0=sc[:, 3:4], in1=sc[:, 4:5])
            gap()
            gap()
            vend = p.op("vector", "tensor_max", out=sc[:, 0:1],
                        in0=sc[:, 5:6], in1=sc[:, 6:7])
        return vend

    def emit_colmax(src_col, dstrow, waits=()):
        tr = p.op("tensor", "transpose", waits=waits,
                  out=pD[0:1, 0:128], in_=src_col, identity=identf[:])
        return p.op("vector", "reduce_max", waits=(("T", tr),),
                    out=dstrow, in_=pD[0:1, 0:128], axis=AX)

    def emit_bcast_col(srcrow, dst_col, waits=()):
        t = mm(pD[:, 0:1], onesf1[0:1, :], srcrow, waits=waits)
        return p.op("scalar", "copy", waits=(("T", t),),
                    out=dst_col, in_=pD[:, 0:1])

    for ch in range(8):
        p.dma("sync", xa[:, ch * T:(ch + 1) * T], xT[128 * ch:128 * (ch + 1), :])
    LX = p.cnt["L"]
    for e in ("vector", "tensor", "scalar"):
        p.wait(e, (("L", LX),))

    reset_scratch()
    vmax = emit_ln("outer")
    vred = emit_colmax(sc[:, 0:1], arst[0:1, 0:1], waits=(("V", vmax),))
    v6 = p.op("vector", "memset", ap=arst[0:1, 1:16], constant=0.0)
    g1 = p.dma("gpsimd", ar_in[0], arst[0:1, :], waits=(("V", max(vred, v6)),))
    p.op("gpsimd", "collective_compute", waits=(("G", g1),), inc="C",
         kind="AllReduce", op=ALU.max, replica_groups=[list(range(NCORE))],
         ins=[ar_in[0]], outs=[ar_out[0]])
    CAR1 = p.cnt["C"]
    p.wait("sync", (("C", CAR1),))
    p.wait("gpsimd", (("C", CAR1),))

    if mkdbg("xn", [D, T]) is not None:
        for ch in range(8):
            p.dma("gpsimd", dbg_out["xn"][128 * ch:128 * (ch + 1), :],
                  xa[:, ch * T:(ch + 1) * T], waits=(("V", vmax),))

    p.fence()

    def emit_wquant():
        reset_scratch()
        wst = sb("wq_st", [128, 2816], F32)
        wq2 = sb("wq_f2", [128, 2816], F32)
        i8b = sb("wq_i8", [128, 2816], I8)
        bfb = sb("wq_bf", [128, 2816], BF16)
        tst = sb("wq_t", [128, 128], BF16)
        rmx = sb("wq_rmx", [128, 4], F32)
        vfree = p.cnt["V"]
        gfree = p.cnt["G"]
        pT = pD[:, 512:1024].bitcast(BF16)
        for wsrc, wdst, nch, wid, swap in ((wg2, wgt, NIC, D, False),
                                           (wu2, wut, NIC, D, False),
                                           (wd2, wdt, 8, DF, True)):
            for chh in range(nch):
                ld = p.dma("sync", wst[:, 0:wid],
                           wsrc[128 * chh:128 * (chh + 1), :],
                           waits=(("V", vfree),))
                vr = p.op("vector", "reduce_max", waits=(("L", ld),),
                          out=rmx[:, 0:1], in_=wst[:, 0:wid], axis=AX,
                          apply_absolute_value=True)
                s1 = p.op("scalar", "activation", waits=(("V", vr),),
                          out=rmx[:, 1:2], in_=rmx[:, 0:1], func=AF.Copy,
                          scale=1.0 / 127.0)
                p.op("vector", "reciprocal", waits=(("S", s1),),
                     out=rmx[:, 2:3], in_=rmx[:, 1:2])
                p.op("vector", "tensor_copy", out=rmx[:, 3:4], in_=rmx[:, 1:2])
                p.op("vector", "tensor_copy", out=rmx[:, 3:4], in_=rmx[:, 1:2])
                p.op("vector", "tensor_scalar_mul", out=wq2[:, 0:wid],
                     in0=wst[:, 0:wid], scalar1=rmx[:, 2:3])
                p.op("vector", "tensor_copy", out=i8b[:, 0:wid],
                     in_=wq2[:, 0:wid])
                vfree = p.op("vector", "tensor_copy", out=wq2[:, 0:wid],
                             in_=i8b[:, 0:wid])
                vbf = p.op("vector", "tensor_scalar_mul", out=bfb[:, 0:wid],
                           in0=wq2[:, 0:wid], scalar1=rmx[:, 1:2])
                if chh == 0 and wsrc is wg2 and mkdbg("wq0", [128, 8]) is not None:
                    p.dma("gpsimd", dbg_out["wq0"][:, 0:4], rmx[:, 0:4],
                          waits=(("V", vbf),))
                if chh == 0 and wsrc is wg2 and mkdbg("bfb0", [128, 1024], BF16) is not None:
                    p.dma("gpsimd", dbg_out["bfb0"], bfb[:, 0:1024],
                          waits=(("V", vbf),))
                for j in range(wid // 128):
                    tt = p.op("tensor", "transpose", out=pT[:, 0:128],
                              in_=bfb[:, 128 * j:128 * (j + 1)],
                              identity=identb[:],
                              waits=(("V", vbf), ("G", gfree)))
                    vt = p.op("vector", "tensor_copy",
                              waits=(("T", tt), ("G", gfree)),
                              out=tst[:], in_=pT[:, 0:128])
                    dstap = (wdst[j, :, 128 * chh:128 * (chh + 1)] if swap
                             else wdst[chh, :, 128 * j:128 * (j + 1)])
                    gfree = p.dma("gpsimd", dstap, tst[:], waits=(("V", vt),))
        return gfree

    GWQ = emit_wquant() if stage >= 1 else p.cnt["G"]
    p.fence()

    def emit_xq_gateup():
        reset_scratch()
        i8b = sb("xq_i8", [128, T], I8)
        qf = sb("xq_f2", [128, T], F32)
        gst2 = [sb("gu_g0", [128, 1024], BF16), sb("gu_g1", [128, 1024], BF16)]
        ust2 = [sb("gu_u0", [128, 1024], BF16), sb("gu_u1", [128, 1024], BF16)]
        gsf = sb("gu_sil", [128, QB], F32)
        hbf = sb("gu_h", [128, QB], F32)
        lcl = p.dma("sync", arst[0:1, 0:2], clipin)
        gld = p.dma("gpsimd", rows[0:1, 0:16], ar_out[0], waits=(("C", CAR1),))
        vg = p.op("vector", "reduce_max", waits=(("G", gld),),
                  out=scr[0:1, 32:33], in_=rows[0:1, 0:16], axis=AX)
        sg = p.op("scalar", "activation", waits=(("L", lcl),),
                  out=scr[0:1, 33:34], in_=arst[0:1, 0:1], func=AF.Sigmoid)
        p.op("vector", "tensor_mul", waits=(("S", sg), ("V", vg)), sep=True,
             out=scr[0:1, 34:35], in0=scr[0:1, 32:33], in1=scr[0:1, 33:34])
        p.op("vector", "tensor_scalar_mul", out=scr[0:1, 34:35], sep=True,
             in0=scr[0:1, 34:35], scalar1=1.0 / 127.0)
        v3 = p.op("vector", "reciprocal", out=scr[0:1, 35:36], sep=True,
                  in_=scr[0:1, 34:35])
        s2 = emit_bcast_col(scr[0:1, 34:35], sc[:, 1:2], waits=(("V", v3),))
        s3 = emit_bcast_col(scr[0:1, 35:36], sc[:, 2:3], waits=(("S", s2),))
        if mkdbg("qsc", [1, 64]) is not None:
            p.dma("gpsimd", dbg_out["qsc"], scr[0:1, :], waits=(("S", s3),))
        vq = 0
        for ch in range(8):
            p.op("vector", "tensor_scalar_mul", waits=(("S", s3),),
                 out=qf[:], in0=xa[:, ch * T:(ch + 1) * T], scalar1=sc[:, 2:3])
            p.op("vector", "tensor_copy", out=i8b[:], in_=qf[:])
            p.op("vector", "tensor_copy", out=qf[:], in_=i8b[:])
            vq = p.op("vector", "tensor_scalar_mul",
                      out=da[:, ch * T:(ch + 1) * T],
                      in0=qf[:], scalar1=sc[:, 1:2])
        if mkdbg("xq", [D, T], BF16) is not None:
            for ch in range(8):
                p.dma("gpsimd", dbg_out["xq"][128 * ch:128 * (ch + 1), :],
                      da[:, ch * T:(ch + 1) * T], waits=(("V", vq),))
        p.op("vector", "memset", ap=sc[:, 0:1], constant=0.0)
        p.op("vector", "memset", ap=sc[:, 14:15], constant=0.0)
        vhm = p.op("vector", "memset", ap=sc[:, 15:16], constant=0.0)
        def gap():
            p.op("vector", "tensor_copy", out=sc[:, 13:14], in_=identf[:, 1:2])
        gfree = p.cnt["G"]
        vfree_g = p.cnt["V"]
        t0g = p.cnt["T"]
        s0g = p.cnt["S"]
        tw_hist = [t0g, t0g]
        sb_hist = [s0g, s0g]
        vm_hist = [vfree_g, vfree_g]
        ld_pair = [None, None]
        for it in range(NIC):
            wp = it % 2
            ldg = p.dma("sync", gst2[wp][:], wgt[it, :, :],
                        waits=(("T", tw_hist[wp]), ("G", GWQ)))
            ldu = p.dma("sync", ust2[wp][:], wut[it, :, :], waits=())
            ld_pair[wp] = ldu
            for qb in range(NQB):
                idx = it * NQB + qb
                par = idx % 2
                cell = 14 + par
                bank = pA if par == 0 else pB
                for h in range(8):
                    mm(bank[:, 0:512], gst2[wp][:, 128 * h:128 * (h + 1)],
                       da[:, h * T + qb * QB:h * T + (qb + 1) * QB],
                       start=(h == 0), stop=(h == 7),
                       waits=(("L", ld_pair[wp]), ("V", max(vq, vhm, vm_hist[par])),
                              ("S", sb_hist[par])) if h == 0 else ())
                tu = None
                for h in range(8):
                    tu = mm(bank[:, 512:1024], ust2[wp][:, 128 * h:128 * (h + 1)],
                            da[:, h * T + qb * QB:h * T + (qb + 1) * QB],
                            start=(h == 0), stop=(h == 7))
                tw_hist[wp] = tu
                ss = p.op("scalar", "activation", waits=(("T", tu), ("V", vfree_g)),
                          out=gsf[:], in_=bank[:, 0:512], func=AF.Silu)
                sb_hist[par] = ss
                vmul = p.op("vector", "tensor_mul", waits=(("S", ss), ("G", gfree)),
                            out=hbf[:], in0=gsf[:], in1=bank[:, 512:1024])
                vm_hist[par] = vmul
                p.op("vector", "tensor_max", out=sc[:, 0:1],
                     in0=sc[:, 0:1], in1=sc[:, cell:cell + 1])
                vfree_g = p.op("vector", "reduce_max", out=sc[:, cell:cell + 1],
                               in_=hbf[:], axis=AX, apply_absolute_value=True)
                gfree = p.dma("gpsimd", h_scr[128 * it:128 * (it + 1),
                                              qb * QB:(qb + 1) * QB],
                              hbf[:], waits=(("V", vmul),))
        gap()
        gap()
        p.op("vector", "tensor_max", out=sc[:, 0:1],
             in0=sc[:, 0:1], in1=sc[:, 14:15])
        gap()
        gap()
        vlast = p.op("vector", "tensor_max", out=sc[:, 0:1],
                     in0=sc[:, 0:1], in1=sc[:, 15:16])
        vr = emit_colmax(sc[:, 0:1], arst[0:1, 0:1], waits=(("V", vlast),))
        v7 = p.op("vector", "memset", ap=arst[0:1, 1:16], constant=0.0)
        g2 = p.dma("gpsimd", ar_in[1], arst[0:1, :],
                   waits=(("V", max(vr, v7)),))
        p.op("gpsimd", "collective_compute", waits=(("G", g2),), inc="C",
             kind="AllReduce", op=ALU.max, replica_groups=[list(range(NCORE))],
             ins=[ar_in[1]], outs=[ar_out[1]])
        p.wait("sync", (("C", p.cnt["C"]),))
        p.wait("gpsimd", (("C", p.cnt["C"]),))

    if stage >= 2:
        emit_xq_gateup()
    CAR2 = p.cnt["C"]
    p.fence()

    def emit_qkv(l, xlb):
        wbuf = sb("qk_w", [128, 8192], BF16)
        wstg = sb("qk_ws", [128, 1024], F32)
        stg = sb("qk_stg", [128, QB], BF16)
        sqs = sb("qk_sq", [128, QB], BF16)
        rot = sb("qk_rot", [128, T], F32)
        lrot = p.dma("sync", rot[:], rotd)
        wsrc = {"q": wq3, "k": wk3, "v": wv3}
        vwcast = p.cnt["V"]
        slast = p.cnt["V"]
        vmaxdone = p.cnt["V"]
        glast = p.cnt["G"]
        vstg = p.cnt["V"]
        for which in ("k", "v", "q"):
            for h in range(8):
                ld = p.dma("sync", wstg[:],
                           wsrc[which][l, 128 * h:128 * (h + 1), :],
                           waits=(("V", vwcast), ("T", p.cnt["T"])))
                vwcast = p.op("vector", "tensor_copy", waits=(("L", ld),),
                              out=wbuf[:, 1024 * h:1024 * (h + 1)], in_=wstg[:])
            bname = {"q": "bq", "k": "bk", "v": "bv"}[which]
            for m in range(8):
                for qb in range(NQB):
                    bank = BANKS[(m * NQB + qb) % 2]
                    tl = None
                    for h in range(8):
                        tl = mm(bank,
                                wbuf[:, 1024 * h + 128 * m:1024 * h + 128 * (m + 1)],
                                xlb[:, h * T + qb * QB:h * T + (qb + 1) * QB],
                                start=(h == 0), stop=(h == 7),
                                waits=(("V", max(slast, vwcast)),) if h == 0 else ())
                    if which == "v":
                        slast = p.op("vector", "tensor_scalar_add", inc="V",
                                     waits=(("T", tl), ("G", glast)),
                                     out=stg[:], in0=bank,
                                     scalar1=col(l, bname, m))
                        glast = p.dma("gpsimd",
                                      ag_in[l][D + 128 * m:D + 128 * (m + 1),
                                               qb * QB:(qb + 1) * QB],
                                      stg[:], waits=(("V", slast),))
                    else:
                        slast = p.op("vector", "tensor_scalar_add", inc="V",
                                     waits=(("T", tl), ("G", glast),
                                            ("L", lrot), ("V", vstg)),
                                     out=stg[:], in0=bank,
                                     scalar1=col(l, bname, m))
                        vr = p.op("vector", "tensor_mul",
                                  out=stg[:], in0=stg[:],
                                  in1=rot[:, qb * QB:(qb + 1) * QB])
                        if which == "q":
                            vstg = p.op("vector", "tensor_copy",
                                        out=qt[:, m * T + qb * QB:m * T + (qb + 1) * QB],
                                        in_=stg[:])
                        else:
                            glast = p.dma("gpsimd",
                                          ag_in[l][128 * m:128 * (m + 1),
                                                   qb * QB:(qb + 1) * QB],
                                          stg[:], waits=(("V", vr),))
                            vstg = vr
                        vsq = p.op("vector", "tensor_mul", out=sqs[:],
                                   in0=stg[:], in1=stg[:])
                        tn = mm(BANKS[2][0:1, :], onesb[:], sqs[:],
                                waits=(("V", vsq),))
                        p.op("vector", "reduce_max", waits=(("T", tn),),
                             out=scr[0:1, 36:37], in_=BANKS[2][0:1, :], axis=AX)
                        dsti = (0 if which == "k" else 8) + m
                        dst = scr[0:1, dsti:dsti + 1]
                        if qb == 0:
                            vmaxdone = p.op("vector", "tensor_copy", out=dst,
                                            sep=True, in_=scr[0:1, 36:37])
                        else:
                            vmaxdone = p.op("vector", "tensor_max", out=dst,
                                            sep=True, in0=dst, in1=scr[0:1, 36:37])
                        vstg = max(vstg, vmaxdone)
        return vmaxdone, glast

    def emit_ag(l, vnorm, glast):
        p.op("vector", "memset", waits=(("V", vnorm),), inc="V",
             ap=nrow[0:1, :], constant=0.0)
        vb = p.op("vector", "tensor_copy", sep=True,
                  out=nrow[0:1, 0:8], in_=scr[0:1, 0:8])
        g3 = p.dma("gpsimd", ag_in[l][2 * D:2 * D + 1, :], nrow[0:1, :],
                   waits=(("V", vb), ("G", glast)))
        p.op("gpsimd", "collective_compute", waits=(("G", g3),), inc="C",
             kind="AllGather", op=ALU.bypass,
             replica_groups=[[0, 1, 2, 3], [4, 5, 6, 7]],
             ins=[ag_in[l]], outs=[ag_out[l]])
        cag = p.cnt["C"]
        p.op("gpsimd", "collective_compute", waits=(("C", cag),), inc="C",
             kind="AllReduce", op=ALU.max, replica_groups=[list(range(NCORE))],
             ins=[bar_in[l]], outs=[bar_out[l]])
        cbar = p.cnt["C"]
        p.wait("sync", (("C", cbar),))
        p.wait("gpsimd", (("C", cbar),))
        return cbar

    def emit_w12cast(l):
        reset_scratch()
        wstg = sb("wc_st", [128, DI], F32)
        wbf = sb("wc_bf", [128, DI], BF16)
        vfree = p.cnt["V"]
        gfree = p.cnt["G"]
        for h in range(8):
            ld = p.dma("sync", wstg[:], w13[l, 128 * h:128 * (h + 1), :],
                       waits=(("V", vfree),))
            vfree = p.op("vector", "tensor_copy", waits=(("L", ld), ("G", gfree)),
                         out=wbf[:], in_=wstg[:])
            gfree = p.dma("gpsimd", w1s[128 * h:128 * (h + 1), :], wbf[:],
                          waits=(("V", vfree),))
        for ic in range(NI1):
            ld = p.dma("sync", wstg[:, 0:1024], w23[l, 128 * ic:128 * (ic + 1), :],
                       waits=(("V", vfree),))
            vfree = p.op("vector", "tensor_copy", waits=(("L", ld), ("G", gfree)),
                         out=wbf[:, 0:1024], in_=wstg[:, 0:1024])
            gfree = p.dma("gpsimd", w2s[128 * ic:128 * (ic + 1), :],
                          wbf[:, 0:1024], waits=(("V", vfree),))
        return gfree

    def emit_attention(l, cag):
        reset_scratch()
        kt = sb("at_k", [128, 8192], BF16)
        dnb = sb("at_dnb", [128, 512], F32)
        vtt = sb("at_vt", [128, T], BF16)
        vt = sb("at_v", [128, 64 * 130], BF16)
        pbuf = sb("at_p", [128, 2048], BF16)
        p.wait("sync", (("C", cag),))
        p.wait("gpsimd", (("C", cag),))
        gn = 0
        for r in range(4):
            gn = p.dma("gpsimd", nrm4[0:1, 8 * r:8 * (r + 1)],
                       ag_out[l][AGR * r + 2 * D:AGR * r + 2 * D + 1, 0:8])
        p.op("vector", "tensor_copy", waits=(("G", gn),),
             out=rows[0:1, 0:32], in_=nrm4[0:1, 0:32])
        vkm = p.op("vector", "reduce_max", sep=True, out=scr[0:1, 0:8],
                   in_=rows[0:1, 0:32].rearrange("a (r j) -> a j r", r=4),
                   axis=AX)
        v1 = p.op("vector", "tensor_mul", waits=(("V", vkm),), sep=True,
                  out=scr[0:1, 16:24], in0=scr[0:1, 0:8], in1=scr[0:1, 8:16])
        s1 = p.op("scalar", "activation", waits=(("V", v1),),
                  out=scr[0:1, 24:32], in_=scr[0:1, 16:24], func=AF.Sqrt)
        v2 = p.op("vector", "tensor_scalar_mul", waits=(("S", s1),), sep=True,
                  out=scr[0:1, 16:24], in0=scr[0:1, 24:32],
                  scalar1=-1.02 / 8.0)
        sprev = p.cnt["S"]
        for pp in range(8):
            sprev = emit_bcast_col(scr[0:1, 16 + pp:17 + pp],
                                   sc[:, 5 + pp:6 + pp],
                                   waits=(("V", v2), ("S", sprev)))
        pT = pD[:, 512:1024].bitcast(BF16)
        texp_prev = [p.cnt["S"]] * 2
        tpv_prev = [p.cnt["T"]] * 2
        vep_done = p.cnt["V"]
        sexp_last = p.cnt["S"]
        klast = 0
        for pp in range(8):
            for r in range(4):
                klast = p.dma("sync", kt[:, 2048 * r:2048 * (r + 1)],
                              ag_out[l][AGR * r + 128 * pp:AGR * r + 128 * (pp + 1), :],
                              waits=(("S", sexp_last),) if r == 0 else ())
            vtrdone = p.op("vector", "memset",
                           ap=vt[:].rearrange("p (c g) -> p c g", g=65)[:, :, 64:65],
                           constant=1.0, waits=(("T", tpv_prev[0]), ("T", tpv_prev[1])))
            for r in range(4):
                lv = p.dma("sync", vtt[:],
                           ag_out[l][AGR * r + D + 128 * pp:AGR * r + D + 128 * (pp + 1), :],
                           waits=(("V", vtrdone),))
                for j in range(16):
                    c = 16 * r + j
                    tt = p.op("tensor", "transpose", out=pT[:, 0:128],
                              in_=vtt[:, 128 * j:128 * (j + 1)],
                              identity=identb[:],
                              waits=(("L", lv), ("V", vtrdone)))
                    vtrdone = p.op("vector", "tensor_copy", waits=(("T", tt),),
                                   out=vt[:, 130 * c:130 * c + 130]
                                   .rearrange("p (g x) -> p g x", g=2)[:, :, 0:64],
                                   in_=pT[:, 0:128]
                                   .rearrange("p (g x) -> p g x", g=2))
            for qb in range(NQB):
                tpv = tpv_prev[1]
                for c in range(NKC):
                    par = c % 2
                    psc = pA if par == 0 else pB
                    mm(psc[:, 0:512], kt[0:64, 128 * c:128 * (c + 1)],
                       qt[0:64, pp * T + qb * QB:pp * T + (qb + 1) * QB],
                       waits=(("S", texp_prev[par]), ("L", klast),
                              ("V", vtrdone)))
                    tsc = mm(psc[:, 512:1024], kt[64:128, 128 * c:128 * (c + 1)],
                             qt[64:128, pp * T + qb * QB:pp * T + (qb + 1) * QB],
                             tp=(64, 0))
                    se = p.op("scalar", "activation",
                              waits=(("T", max(tsc, tpv_prev[par])),),
                              out=pbuf[:, 1024 * par:1024 * (par + 1)],
                              in_=psc[:, 0:1024], func=AF.Exp,
                              scale=0.125, bias=sc[:, 5 + pp:6 + pp])
                    texp_prev[par] = se
                    sexp_last = se
                    w0 = [("S", se)]
                    if c == 0:
                        w0.append(("V", vep_done))
                    mm(pC[0:65, 0:512], vt[:, 130 * c:130 * c + 65],
                       pbuf[:, 1024 * par:1024 * par + 512],
                       start=(c == 0), stop=(c == NKC - 1), waits=tuple(w0))
                    tpv = mm(pC[0:65, 512:1024], vt[:, 130 * c + 65:130 * c + 130],
                             pbuf[:, 1024 * par + 512:1024 * (par + 1)],
                             start=(c == 0), stop=(c == NKC - 1))
                    tpv_prev[par] = tpv
                v3 = p.op("vector", "reciprocal", waits=(("T", tpv),),
                          out=rows[0:1, 0:512], in_=pC[64:65, 0:512])
                t4 = mm(pD[0:64, 0:512], onesf1[0:1, 0:64], rows[0:1, 0:512],
                        waits=(("V", v3),))
                vb1 = p.op("vector", "tensor_copy", waits=(("T", t4),),
                           out=dnb[0:64, :], in_=pD[0:64, 0:512])
                va = p.op("vector", "tensor_mul",
                          out=da[0:64, pp * T + qb * QB:pp * T + (qb + 1) * QB],
                          in0=pC[0:64, 0:512], in1=dnb[0:64, :])
                v4 = p.op("vector", "reciprocal", sep=True,
                          out=rows[0:1, 512:1024], in_=pC[64:65, 512:1024])
                t5 = mm(pD[0:64, 0:512], onesf1[0:1, 0:64], rows[0:1, 512:1024],
                        waits=(("V", v4),))
                vb2 = p.op("vector", "tensor_copy", waits=(("T", t5),),
                           out=dnb[0:64, :], in_=pD[0:64, 0:512])
                vep_done = p.op("vector", "tensor_mul",
                                out=da[64:128, pp * T + qb * QB:pp * T + (qb + 1) * QB],
                                in0=pC[0:64, 512:1024], in1=dnb[0:64, :])
        return vep_done

    def emit_wo(l, vatt):
        if l == 0 and mkdbg("att0", [D, T], BF16) is not None:
            for ch in range(8):
                p.dma("gpsimd", dbg_out["att0"][128 * ch:128 * (ch + 1), :],
                      da[:, ch * T:(ch + 1) * T], waits=(("V", vatt),))
        reset_scratch()
        wbuf = sb("wo_w", [128, 8192], BF16)
        wstg = sb("wo_ws", [128, 1024], F32)
        vw = p.cnt["V"]
        for h in range(8):
            ld = p.dma("sync", wstg[:], wo3[l, 128 * h:128 * (h + 1), :],
                       waits=(("V", vw),))
            vw = p.op("vector", "tensor_copy", waits=(("L", ld),),
                      out=wbuf[:, 1024 * h:1024 * (h + 1)], in_=wstg[:])
        vv = [max(vw, vatt)] * 4
        for m in range(8):
            for qb in range(NQB):
                bi = (m * NQB + qb) % 4
                bank = BANKS[bi]
                tl = None
                for h in range(8):
                    tl = mm(bank,
                            wbuf[:, 1024 * h + 128 * m:1024 * h + 128 * (m + 1)],
                            da[:, h * T + qb * QB:h * T + (qb + 1) * QB],
                            start=(h == 0), stop=(h == 7),
                            waits=(("V", vv[bi]),) if h == 0 else ())
                vv[bi] = p.op("vector", "tensor_add", waits=(("T", tl),),
                              out=xa[:, m * T + qb * QB:m * T + (qb + 1) * QB],
                              in0=xa[:, m * T + qb * QB:m * T + (qb + 1) * QB],
                              in1=bank)
        return max(vv)

    def emit_ffn(l):
        if l == 1 and mkdbg("xm1", [D, T]) is not None:
            for ch in range(8):
                p.dma("gpsimd", dbg_out["xm1"][128 * ch:128 * (ch + 1), :],
                      xa[:, ch * T:(ch + 1) * T])
        if l == 0 and mkdbg("xm0", [D, T]) is not None:
            for ch in range(8):
                p.dma("gpsimd", dbg_out["xm0"][128 * ch:128 * (ch + 1), :],
                      xa[:, ch * T:(ch + 1) * T])
        reset_scratch()
        emit_ln("affine", l, "ln2_g", "ln2_b", dst_bf=qt)
        vln = p.cnt["V"]
        reset_scratch()
        w1b = [sb("f_w1a", [128, 4096], BF16), sb("f_w1b", [128, 4096], BF16)]
        w2b = [sb("f_w2a", [128, 1024], BF16), sb("f_w2b", [128, 1024], BF16)]
        vhid = p.cnt["V"]
        shid = p.cnt["S"]
        t0f = p.cnt["T"]
        tw1 = [t0f, t0f]
        tw2 = [t0f, t0f]
        for qb in range(NQB):
            for mg in range(8):
                kw1 = qb * 8 + mg
                wp = kw1 % 2
                lw = 0
                for h in range(8):
                    lw = p.dma("sync", w1b[wp][:, 512 * h:512 * (h + 1)],
                               w1s[128 * h:128 * (h + 1),
                                   512 * mg:512 * (mg + 1)],
                               waits=(("T", tw1[wp]), ("V", vln)) if h == 0 else ())
                for mt_in in range(4):
                    mt = mg * 4 + mt_in
                    bank = BANKS[mt % 2]
                    tl = None
                    for h in range(8):
                        w1w = (("L", lw), ("S", shid))
                        if mg == 0 and mt_in == 0:
                            w1w = w1w + (("V", vhid),)
                        tl = mm(bank,
                                w1b[wp][:, 512 * h + 128 * mt_in:512 * h + 128 * (mt_in + 1)],
                                qt[:, h * T + qb * QB:h * T + (qb + 1) * QB],
                                start=(h == 0), stop=(h == 7),
                                waits=w1w if h == 0 else ())
                    shid = p.op("scalar", "activation", waits=(("T", tl),),
                                out=da[:, 512 * mt:512 * (mt + 1)], in_=bank,
                                func=AF.Gelu_apprx_sigmoid,
                                bias=col(l, "b1", mt), scale=1.0)
                tw1[wp] = tl
            if l == 0 and qb == 0 and mkdbg("hid0", [128, 8 * T], BF16) is not None:
                p.dma("gpsimd", dbg_out["hid0"], da[:], waits=(("S", shid),))
            tl2 = [None] * 8
            for ic in range(NI1):
                wp2 = ic % 2
                lw2 = p.dma("sync", w2b[wp2][:], w2s[128 * ic:128 * (ic + 1), :],
                            waits=(("T", tw2[wp2]), ("V", vln)))
                for mo in range(8):
                    tl2[mo] = mm(BANKS[mo], w2b[wp2][:, 128 * mo:128 * (mo + 1)],
                                 da[:, 512 * ic:512 * (ic + 1)],
                                 start=(ic == 0), stop=(ic == NI1 - 1),
                                 waits=(("L", lw2), ("S", shid),
                                        ("V", vhid)) if mo == 0 else ())
                tw2[wp2] = tl2[7]
            for mo in range(8):
                vhid = p.op("vector", "tensor_add", waits=(("T", tl2[mo]),),
                            out=xa[:, mo * T + qb * QB:mo * T + (qb + 1) * QB],
                            in0=xa[:, mo * T + qb * QB:mo * T + (qb + 1) * QB],
                            in1=BANKS[mo])
        return vhid

    def emit_down():
        p.fence()
        reset_scratch()
        hst2 = [sb("d_h0", [128, QB], F32), sb("d_h1", [128, QB], F32)]
        hf2 = sb("d_f2", [128, QB], F32)
        ost = sb("d_o", [128, QB], F32)
        i8b = sb("d_i8", [128, QB], I8)
        hqb2 = [sb("d_hq0", [128, QB], BF16), sb("d_hq1", [128, QB], BF16)]
        wdb = [sb("d_wd0", [128, 1024], BF16), sb("d_wd1", [128, 1024], BF16)]
        gld = p.dma("gpsimd", rows[0:1, 0:16], ar_out[1], waits=(("C", CAR2),))
        vg = p.op("vector", "reduce_max", waits=(("G", gld),),
                  out=scr[0:1, 40:41], in_=rows[0:1, 0:16], axis=AX)
        lcl = p.dma("sync", arst[0:1, 0:2], clipin)
        sg = p.op("scalar", "activation", waits=(("L", lcl),),
                  out=scr[0:1, 41:42], in_=arst[0:1, 1:2], func=AF.Sigmoid)
        p.op("vector", "tensor_mul", waits=(("S", sg), ("V", vg)), sep=True,
             out=scr[0:1, 42:43], in0=scr[0:1, 40:41], in1=scr[0:1, 41:42])
        p.op("vector", "tensor_scalar_mul", out=scr[0:1, 42:43], sep=True,
             in0=scr[0:1, 42:43], scalar1=1.0 / 127.0)
        v3 = p.op("vector", "reciprocal", out=scr[0:1, 43:44], sep=True,
                  in_=scr[0:1, 42:43])
        sa_ = emit_bcast_col(scr[0:1, 42:43], sc[:, 3:4], waits=(("V", v3),))
        sb_ = emit_bcast_col(scr[0:1, 43:44], sc[:, 4:5], waits=(("S", sa_),))
        if mkdbg("dsc", [1, 64]) is not None:
            p.dma("gpsimd", dbg_out["dsc"], scr[0:1, :], waits=(("S", sb_),))
        vq = p.cnt["V"]
        gof = p.cnt["G"]
        t0 = p.cnt["T"]
        thist = [t0, t0]
        tlprev = t0
        for qb in range(NQB):
            tl2 = [None] * 8
            for ic in range(NIC):
                k = qb * NIC + ic
                par = k % 2
                hqb = hqb2[par]
                lh = p.dma("sync", hst2[par][:],
                           h_scr[128 * ic:128 * (ic + 1), qb * QB:(qb + 1) * QB],
                           waits=(("V", vq),))
                lwd = p.dma("sync", wdb[par][:], wdt[ic, :, :],
                            waits=(("T", tlprev),))
                p.op("vector", "tensor_scalar_mul",
                     waits=(("L", lh), ("S", sb_), ("T", thist[par])),
                     out=hf2[:], in0=hst2[par][:], scalar1=sc[:, 4:5])
                p.op("vector", "tensor_copy", out=i8b[:], in_=hf2[:])
                p.op("vector", "tensor_copy", out=hf2[:], in_=i8b[:])
                vq = p.op("vector", "tensor_scalar_mul", out=hqb[:],
                          in0=hf2[:], scalar1=sc[:, 3:4])
                if qb == 0 and ic == 0 and mkdbg("hq00", [128, QB], BF16) is not None:
                    p.dma("gpsimd", dbg_out["hq00"], hqb[:], waits=(("V", vq),))
                if mkdbg("hqall", [128, NQB * NIC * QB], BF16) is not None:
                    p.dma("gpsimd", dbg_out["hqall"][:, k * QB:(k + 1) * QB],
                          hqb[:], waits=(("V", vq),))
                for mo in range(8):
                    tl2[mo] = mm(BANKS[mo], wdb[par][:, 128 * mo:128 * (mo + 1)],
                                 hqb[:], start=(ic == 0), stop=(ic == NIC - 1),
                                 waits=(("L", lwd), ("V", vq)) if mo == 0 else ())
                thist[par] = tl2[7]
                tlprev = tl2[7]
            for mo in range(8):
                vv = p.op("vector", "tensor_copy",
                          waits=(("T", tl2[mo]), ("G", gof)),
                          out=ost[:], in_=BANKS[mo])
                gof = p.dma("gpsimd",
                            ffn_scr[128 * mo:128 * (mo + 1), qb * QB:(qb + 1) * QB],
                            ost[:], waits=(("V", vv),))
                vq = max(vq, vv)
        return gof

    def emit_layer(l):
        if l == 1 and mkdbg("xl0", [D, T]) is not None:
            for ch in range(8):
                p.dma("gpsimd", dbg_out["xl0"][128 * ch:128 * (ch + 1), :],
                      xa[:, ch * T:(ch + 1) * T])
        reset_scratch()
        emit_ln("affine", l, "ln1_g", "ln1_b", dst_bf=da)
        reset_scratch()
        vn, gl = emit_qkv(l, da)
        cag = emit_ag(l, vn, gl)
        if l == 0:
            if stage >= 3:
                emit_w12cast(0)
        else:
            if stage >= 6:
                emit_down()
        p.fence()
        if stage >= (4 if l == 0 else 7):
            vatt = emit_attention(l, cag)
            emit_wo(l, vatt)
            p.fence()
        if stage >= (5 if l == 0 else 8):
            if l == 1:
                emit_w12cast(1)
                p.fence()
            emit_ffn(l)
            p.fence()

    if stage >= 3:
        emit_layer(0)
    if stage >= 6:
        emit_layer(1)

    reset_scratch()
    xst = sb("p2_x", [128, T], F32)
    fst = sb("p2_f", [128, T], F32)
    gout = 0
    vdone = p.cnt["V"]
    for ch in range(8):
        l1 = p.dma("sync", xst[:], xT[128 * ch:128 * (ch + 1), :],
                   waits=(("V", vdone), ("G", gout)))
        l2 = p.dma("sync", fst[:], ffn_scr[128 * ch:128 * (ch + 1), :],
                   waits=(("G", p.cnt["G"]),))
        p.op("vector", "tensor_add", waits=(("L", l2),),
             out=xst[:], in0=xst[:], in1=xa[:, ch * T:(ch + 1) * T])
        vdone = p.op("vector", "tensor_add", out=xst[:], in0=xst[:], in1=fst[:])
        gout = p.dma("gpsimd", oT[128 * ch:128 * (ch + 1), :], xst[:],
                     waits=(("V", vdone),))
    p.wait("sync", (("G", gout),))

    for nm in ("xa_f",):
        if nm in dbg:
            dd = mkdbg(nm, [D, T])
            for ch in range(8):
                p.dma("gpsimd", dd[128 * ch:128 * (ch + 1), :],
                      xa[:, ch * T:(ch + 1) * T])
    if "h_d" in dbg:
        dd = mkdbg("h_d", [DF, T])
        p.dma("gpsimd", dd, h_scr, waits=(("G", p.cnt["G"]),))
    if "wgt_d" in dbg:
        dd = mkdbg("wgt_d", [NIC, 128, 1024], BF16)
        p.dma("gpsimd", dd, wgt, waits=(("G", p.cnt["G"]),))
    if "wut_d" in dbg:
        dd = mkdbg("wut_d", [NIC, 128, 1024], BF16)
        p.dma("gpsimd", dd, wut, waits=(("G", p.cnt["G"]),))
    if "f_d" in dbg:
        dd = mkdbg("f_d", [D, T])
        p.dma("gpsimd", dd, ffn_scr)
    for nm in ("qt_d", "da_d"):
        if nm in dbg:
            dd = mkdbg(nm, [D, T], BF16)
            srcb = qt if nm == "qt_d" else da
            for ch in range(8):
                p.dma("gpsimd", dd[128 * ch:128 * (ch + 1), :],
                      srcb[:, ch * T:(ch + 1) * T])

    assert ar.size <= 188 * 1024, f"sbuf over budget: {ar.size}" 
    p.check()
    with nc.Block() as block:
        p.build(block)
    return nc, sorted(dbg_out.keys())


_cache = {}


def kernel(**inputs):
    x = np.asarray(inputs["L0_attn_x"])
    odt = x.dtype

    key = (STAGE, tuple(sorted(DBG)))
    if key not in _cache:
        _cache[key] = build_program(STAGE, DBG)
    nc, dbg_names = _cache[key]

    def f32(a):
        return np.ascontiguousarray(np.asarray(a, dtype=np.float32))

    cx = float(np.asarray(inputs["L0_ffn_clip_x"]).ravel()[0])
    cd = float(np.asarray(inputs["L0_ffn_clip_down"]).ravel()[0])
    shared = {
        "wq3": f32(inputs["L0_attn_wq"]), "wk3": f32(inputs["L0_attn_wk"]),
        "wv3": f32(inputs["L0_attn_wv"]), "wo3": f32(inputs["L0_attn_wo"]),
        "w13": f32(inputs["L0_attn_w1"]), "w23": f32(inputs["L0_attn_w2"]),
        "wg2": f32(inputs["L0_ffn_w_gate"]), "wu2": f32(inputs["L0_ffn_w_up"]),
        "wd2": f32(inputs["L0_ffn_w_down"]),
        "identbd": np.eye(128, dtype=ml_dtypes.bfloat16),
        "identfd": np.eye(128, dtype=np.float32),
        "onesbd": np.ones((128, 1), dtype=ml_dtypes.bfloat16),
        "onesf1d": np.ones((1, 128), dtype=np.float32),
        "clipin": np.array([[cx, cd]], dtype=np.float32),
    }
    bias_host = np.zeros((128, NBC), np.float32)
    src = {"ln1_g": f32(inputs["L0_attn_ln1_g"]), "ln1_b": f32(inputs["L0_attn_ln1_b"]),
           "ln2_g": f32(inputs["L0_attn_ln2_g"]), "ln2_b": f32(inputs["L0_attn_ln2_b"]),
           "bq": f32(inputs["L0_attn_bq"]), "bk": f32(inputs["L0_attn_bk"]),
           "bv": f32(inputs["L0_attn_bv"]), "b1": f32(inputs["L0_attn_b1"])}
    for (l, nm, c), cc in BIAS_COLS.items():
        bias_host[:, cc] = src[nm][l, 128 * c:128 * (c + 1)]
    shared["biasd"] = bias_host

    in_maps = []
    for ci in range(NCORE):
        bb, qi = ci // 4, ci % 4
        t0 = qi * T
        m = dict(shared)
        m["xT"] = np.ascontiguousarray(f32(x[bb, t0:t0 + T, :]).T)
        m["rotd"] = host_rot_table(t0)
        in_maps.append(m)

    res = run_bass_kernel_spmd(nc, in_maps, core_ids=list(range(NCORE)))

    out = np.empty((B, S, D), np.float32)
    for ci in range(NCORE):
        bb, qi = ci // 4, ci % 4
        out[bb, qi * T:(qi + 1) * T, :] = res.results[ci]["oT"].T
    kernel._dbg = [{k: res.results[ci].get("dbg_" + k) for k in dbg_names}
                   for ci in range(NCORE)]
    return out.astype(odt)
